# version 01
# baseline (speedup 1.0000x reference)
import numpy as np
import ml_dtypes
from contextlib import ExitStack

import concourse.bass as bass
import concourse.mybir as mybir
from concourse.bass_utils import run_bass_kernel_spmd

F32 = mybir.dt.float32
BF16 = mybir.dt.bfloat16
I32 = mybir.dt.int32
AF = mybir.ActivationFunctionType
ALU = mybir.AluOpType
AX = mybir.AxisListType
BF16NP = ml_dtypes.bfloat16


NLANES = 8
QUEUES = ("qy", "qg", "qs")
CHAINS = ("t", "v", "s", "g", "y") + tuple(
    f"{q}{l}" for q in QUEUES for l in range(NLANES))
ENGINES = ("tensor", "vector", "scalar", "gpsimd", "sync")
CHAIN_ENGINE = {"t": "tensor", "v": "vector", "s": "scalar", "g": "gpsimd",
                "y": "sync"}
for _l in range(NLANES):
    CHAIN_ENGINE[f"qy{_l}"] = "sync"
    CHAIN_ENGINE[f"qg{_l}"] = "gpsimd"
    CHAIN_ENGINE[f"qs{_l}"] = "scalar"
DMA_CHAINS = tuple(f"{q}{l}" for q in QUEUES for l in range(NLANES))
_DMA_SET = set(DMA_CHAINS)


class BB:
    __slots__ = ("h", "name", "writer", "readers")

    def __init__(self, handle, name):
        self.h = handle
        self.name = name
        self.writer = None
        self.readers = {}


class Op:
    __slots__ = ("chain", "idx", "emit", "waits", "marked", "units", "kn")

    def __init__(self, chain, idx, emit, units):
        self.chain = chain
        self.idx = idx
        self.emit = emit
        self.waits = []
        self.marked = False
        self.units = units
        self.kn = None


b_guard = {}


class KB:

    def __init__(self, nc):
        self.nc = nc
        self.stack = ExitStack()
        self.ops = {c: [] for c in CHAINS}
        self.sched = {e: [] for e in ENGINES}
        self.known = {c: {c2: 0 for c2 in CHAINS} for c in CHAINS}
        self._names = set()
        self._qrr = {q: 0 for q in QUEUES}
        self._alloc_seq = 0

    def _uname(self, name):
        base, i = name, 0
        while name in self._names:
            i += 1
            name = f"{base}_{i}"
        self._names.add(name)
        return name

    def sb(self, name, shape, dtype):
        name = self._uname(name)
        g = self.nc.sbuf_tensor(name, list(shape), dtype)
        h = g.__enter__()
        b = BB(h, name)
        b_guard[id(b)] = (self._alloc_seq, g)
        self._alloc_seq += 1
        return b

    def ps(self, name, shape, dtype=F32):
        name = self._uname(name)
        g = self.nc.psum_tensor(name, list(shape), dtype)
        h = g.__enter__()
        b = BB(h, name)
        b_guard[id(b)] = (self._alloc_seq, g)
        self._alloc_seq += 1
        return b

    def free(self, *bbs):
        items = [(b_guard[id(b)][0], id(b)) for b in bbs if id(b) in b_guard]
        for _, bid in sorted(items, reverse=True):
            _, g = b_guard.pop(bid)
            g.__exit__(None, None, None)

    def barrier(self):
        lasts = [(c, len(self.ops[c]) - 1) for c in CHAINS
                 if self.ops[c] and c not in ("t", "v", "s", "g", "y")]
        elasts = {c: len(self.ops[c]) - 1 for c in ("t", "v", "s", "g", "y")
                  if self.ops[c]}
        nop_emit = {
            "t": lambda e: e.nop(), "v": lambda e: e.engine_nop(),
            "s": lambda e: e.nop(), "g": lambda e: e.nop(),
            "y": lambda e: e.nop(),
        }
        new_ops = []
        for c in ("t", "v", "s", "g", "y"):
            idx = len(self.ops[c])
            o = Op(c, idx, nop_emit[c], 1)
            o.waits = list(lasts) + [(c2, p) for c2, p in elasts.items() if c2 != c]
            for (c2, p) in o.waits:
                self.ops[c2][p].marked = True
            o.kn = {c2: len(self.ops[c2]) for c2 in CHAINS}
            o.kn[c] = idx + 1
            self.ops[c].append(o)
            self.sched[CHAIN_ENGINE[c]].append(o)
            new_ops.append(o)
        full = {c2: len(self.ops[c2]) for c2 in CHAINS}
        for c in CHAINS:
            self.known[c] = dict(full)

    def dram(self, name, shape, dtype, kind="Internal"):
        name = self._uname(name)
        h = self.nc.dram_tensor(name, list(shape), dtype, kind=kind)
        return BB(h, name)

    def op(self, chain, emit, reads=(), writes=(), raw_self=True):
        idx = len(self.ops[chain])
        units = 16 if chain in DMA_CHAINS else 1
        o = Op(chain, idx, emit, units)
        kn = self.known[chain]
        need = {}

        def _need(c, p):
            if p is None:
                return
            if need.get(c, -1) < p:
                need[c] = p

        for b in reads:
            if b.writer is not None:
                _need(*b.writer)
        for b in writes:
            if b.writer is not None:
                wc, wp = b.writer
                if not (wc == chain and chain == "t"):
                    _need(wc, wp)
            for rc, rp in b.readers.items():
                if rc == chain and chain not in DMA_CHAINS:
                    continue
                _need(rc, rp)

        for c, p in sorted(need.items()):
            if c == chain and chain == "t":
                continue
            if c == chain and not raw_self and chain not in DMA_CHAINS:
                continue
            if kn[c] >= p + 1:
                continue
            prod = self.ops[c][p]
            prod.marked = True
            o.waits.append((c, p))
            pkn = prod.kn
            for c2 in CHAINS:
                if pkn[c2] > kn[c2]:
                    kn[c2] = pkn[c2]
            if kn[c] < p + 1:
                kn[c] = p + 1

        o.kn = dict(kn)
        o.kn[chain] = max(o.kn[chain], idx + 1)
        self.ops[chain].append(o)
        self.sched[CHAIN_ENGINE[chain]].append(o)
        for b in reads:
            if b.readers.get(chain, -1) < idx:
                b.readers[chain] = idx
        for b in writes:
            b.writer = (chain, idx)
            b.readers = {}
        return o

    def dma(self, queue, out_bb, out_ap, in_bb, in_ap, **kw):
        lane = self._qrr[queue]
        self._qrr[queue] = (lane + 1) % NLANES
        chain = f"{queue}{lane}"

        def emit(eng):
            return eng.dma_start(out=out_ap, in_=in_ap, **kw)
        return self.op(chain, emit, reads=(in_bb,) if in_bb else (),
                       writes=(out_bb,) if out_bb else ())

    def finalize(self):
        last = []
        for c in CHAINS:
            if self.ops[c]:
                if not (c == "y" and self.ops[c]):
                    pass
                self.ops[c][-1].marked = True
                last.append((c, len(self.ops[c]) - 1))
        for c in DMA_CHAINS:
            for o in self.ops[c]:
                o.marked = True
        prefix = {}
        for c in CHAINS:
            acc, pl = 0, []
            for o in self.ops[c]:
                if o.marked:
                    acc += o.units
                pl.append(acc)
            prefix[c] = pl

        sems = {c: self.stack.enter_context(self.nc.semaphore(f"sem_{c}"))
                for c in CHAINS}

        block = self.stack.enter_context(self.nc.Block())

        def engine_body(eng_name):
            def body(eng):
                for o in self.sched[eng_name]:
                    if o.chain in _DMA_SET and o.idx > 0:
                        eng.wait_ge(sems[o.chain], prefix[o.chain][o.idx] - o.units)
                    for (c, p) in o.waits:
                        eng.wait_ge(sems[c], prefix[c][p])
                    inst = o.emit(eng)
                    if o.marked:
                        inst.then_inc(sems[o.chain], o.units)
                if eng_name == "sync":
                    for (c, p) in last:
                        if c != "y" and prefix[c]:
                            eng.wait_ge(sems[c], prefix[c][-1])
            return body

        block.sync(engine_body("sync"))
        block.tensor(engine_body("tensor"))
        block.vector(engine_body("vector"))
        block.scalar(engine_body("scalar"))
        block.gpsimd(engine_body("gpsimd"))
        self.stack.close()

    def stats(self):
        return {c: len(self.ops[c]) for c in CHAINS}


D = 1024
DCH = D // 128
SEQ = 4096
NH = 16
EH = 64
WSWA = 128
CLIN = 128
EPS = 1e-5

TILE_T = 512


def _chunks(n, sz):
    return [(i, min(i + sz, n)) for i in range(0, n, sz)]


class L0:

    def __init__(self, kb, nown=2048, npfx=2048):
        self.kb = kb
        self.nc = kb.nc
        self.nown = nown
        self.npfx = npfx
        self.ntot = nown + npfx
        self.nch = nown // 128
        self.pch = npfx // 128
        self.tch = self.ntot // 128
        kb_, nc = kb, kb.nc
        inp = self.inp = {}

        def ext(name, shape, dtype=F32):
            b = kb.dram(name, shape, dtype, kind="ExternalInput")
            inp[name] = b
            return b

        self.ext = ext
        ext("xq", [nown, D])
        ext("xp", [max(npfx, 128), D])
        ext("c_id128", [128, 128])
        ext("hflag", [128, 1])
        ext("w_g1b1", [128, 2 * DCH])
        ext("w_centT", [D, 8])
        ext("w_qkv_a", [D, 3 * D])
        ext("w_qkv_s", [D, 3 * D])
        ext("b_qkv_a_fm", [128, 2 * DCH])
        ext("b_qkv_s_fm", [128, 2 * DCH])
        ext("b_v_a", [1, D])
        ext("b_v_s", [1, D])
        ext("m_swa_first", [128, 256])
        ext("m_swa_band", [128, 256])
        ext("m_tril", [128, 128])
        ext("w_p_a", [D, D])
        ext("w_o_s", [D, D])
        ext("b_po_fm", [128, DCH])
        ext("b_p_a_fm", [128, DCH])
        ext("b_o_s_fm", [128, DCH])
        ext("w_o", [D, D])
        ext("b_o", [1, D])
        ext("w_gmbm", [128, 2 * DCH])
        ext("w_1", [D, 4 * D])
        ext("b_1m_fm", [128, 4 * D // 128])
        ext("w_2", [4 * D, D])
        ext("b_2m", [1, D])
        ext("w_moe1T", [8, D, 4096])
        ext("w_moe2T", [8, 4096, D])
        ext("w_lng", [8, 128, DCH])
        ext("w_lnb", [8, 128, DCH])
        ext("b_moe1_fm", [8, 128, 32])
        ext("b_moe2", [8, 1, D])

        self.d_xn0 = kb.dram("d_xn0", [nown, D], BF16)
        self.d_ain = kb.dram("d_ain", [self.tch, DCH, 128, 128], BF16)
        self.d_xhat = kb.dram("d_xhat", [self.nch, DCH, 128, 128], BF16)
        self.d_swa = kb.dram("d_swa", [self.nch, DCH, 128, 128], BF16)
        self.d_lin = kb.dram("d_lin", [self.nch, DCH, 128, 128], BF16)
        self.d_ab = kb.dram("d_ab", [self.nch, DCH, 128, 128], BF16)
        self.d_x1 = kb.dram("d_x1", [nown, D], BF16)
        self.d_h0 = kb.dram("d_h0", [self.nch, DCH, 128, 128], BF16)
        self.d_s = kb.dram("d_s", [nown, D], BF16)
        self.d_g = kb.dram("d_g", [nown, D], BF16)
        self.out_xnew = kb.dram("xnew", [nown, D], F32, kind="ExternalOutput")

        self.id128 = kb.sb("id128", [128, 128], BF16)
        self.ones_r = kb.sb("ones_r", [1, 512], BF16)
        self.epst = kb.sb("epst", [128, 1], F32)
        self.zeros8 = kb.sb("zeros8", [128, 8], F32)
        self.w_tm = kb.sb("w_tm", [128, self.nch * 8], F32)
        self.csumb = kb.sb("csumb", [128, 8], F32)
        self.hflag_t = kb.sb("hflag_t", [128, 1], F32)

        self.psA = [kb.ps(f"psA{i}", [128, 512], F32) for i in range(4)]
        self.psB = [kb.ps(f"psB{i}", [128, 512], F32) for i in range(2)]
        self.psT = [kb.ps(f"psT{i}", [128, 1024], BF16) for i in range(2)]

    def load_fm(self, q, sbuf_bb, colsj, dram_bb, c0, nch_ld):
        for j in range(DCH):
            self.kb.dma(q, sbuf_bb,
                        sbuf_bb.h[:, j * colsj: j * colsj + nch_ld * 128]
                        .rearrange("p (tc c) -> p tc c", tc=nch_ld),
                        dram_bb,
                        dram_bb.h.ap()[c0:c0 + nch_ld, j, :, :]
                        .rearrange("tc p c -> p tc c"))

    def emit_init(self):
        kb = self.kb
        self.id128f = kb.sb("idf", [128, 128], F32)
        kb.dma("qg", self.id128, self.id128.h[:, :], self.inp["c_id128"],
               self.inp["c_id128"].h.ap()[:, :])
        kb.dma("qy", self.id128f, self.id128f.h[:, :], self.inp["c_id128"],
               self.inp["c_id128"].h.ap()[:, :])
        kb.dma("qy", self.hflag_t, self.hflag_t.h[:, :], self.inp["hflag"],
               self.inp["hflag"].h.ap()[:, :])
        kb.op("v", lambda e: e.memset(self.ones_r.h[:, :], 1.0), writes=(self.ones_r,))
        kb.op("v", lambda e: e.memset(self.epst.h[:, :], EPS), writes=(self.epst,))
        kb.op("v", lambda e: e.memset(self.zeros8.h[:, :], 0.0), writes=(self.zeros8,))

        centT = kb.sb("centT", [128, DCH * 8], BF16)
        kb.dma("qg", centT, centT.h[:, :].rearrange("p (j c) -> p j c", j=DCH),
               self.inp["w_centT"],
               self.inp["w_centT"].h.ap()[:, :].rearrange("(j p) c -> p j c", p=128))
        self.centT = centT
        centTf = kb.sb("centTf", [128, DCH * 8], F32)
        kb.dma("qy", centTf, centTf.h[:, :].rearrange("p (j c) -> p j c", j=DCH),
               self.inp["w_centT"],
               self.inp["w_centT"].h.ap()[:, :].rearrange("(j p) c -> p j c", p=128))
        self.centTf = centTf
        onec = kb.sb("onec", [128, 1], BF16)
        kb.op("v", lambda e: e.memset(onec.h[:, :], 1.0), writes=(onec,))
        ps = self.psB[0]
        for j in range(DCH):
            kb.op("t", (lambda jj: lambda e: e.matmul(
                ps.h[0:1, 0:8], onec.h[:, 0:1], centT.h[:, jj * 8:(jj + 1) * 8],
                start=(jj == 0), stop=(jj == DCH - 1)))(j),
                reads=(onec, centT), writes=(ps,))
        csr = kb.sb("csr", [1, 8], BF16)
        kb.op("v", lambda e: e.tensor_copy(csr.h[:, :], ps.h[0:1, 0:8]),
              reads=(ps,), writes=(csr,))
        kb.op("t", lambda e: e.matmul(ps.h[:, 8:16], self.ones_r.h[0:1, 0:128],
                                      csr.h[0:1, :], start=True, stop=True),
              reads=(self.ones_r, csr), writes=(ps,))
        kb.op("v", lambda e: e.tensor_copy(self.csumb.h[:, :], ps.h[:, 8:16]),
              reads=(ps,), writes=(self.csumb,))

    def emit_a1(self, limit=None, parts=("store", "tr", "ain", "xhat", "gate")):
        kb = self.kb
        parts = set(parts)
        xt = [kb.sb(f"a1x{i}", [128, D], F32) for i in range(2)]
        xnf = [kb.sb(f"a1nf{i}", [128, D], F32) for i in range(2)]
        fmf = [kb.sb(f"a1ff{i}", [128, 128], F32) for i in range(2)]
        xnt = [kb.sb(f"a1n{i}", [128, D], BF16) for i in range(2)]
        xht = [kb.sb(f"a1h{i}", [128, D], BF16) for i in range(2)]
        sq = [kb.sb(f"a1q{i}", [128, D], F32) for i in range(2)]
        st = [kb.sb(f"a1s{i}", [128, 12], F32) for i in range(2)]
        fmst = [kb.sb(f"a1f{i}", [128, DCH * 128], BF16) for i in range(2)]
        raws = [kb.sb(f"a1r{i}", [128, DCH * 128], BF16) for i in range(2)]
        affs = [kb.sb(f"aff{i}", [128, 40], F32) for i in range(2)]
        g1 = kb.sb("g1t", [128, 2 * DCH], F32)
        kb.dma("qy", g1, g1.h[:, :], self.inp["w_g1b1"], self.inp["w_g1b1"].h.ap()[:, :])

        def stats(eng_bufs, src_bb, src_ap, mu_sl, r_sl, sd_sl, i):
            s, q = st[i], sq[i]
            kb.op("v", lambda e: e.reduce_sum(s.h[:, mu_sl:mu_sl + 1], src_ap, axis=AX.X),
                  reads=(src_bb,), writes=(s,))
            kb.op("s", lambda e: e.activation(q.h[:, :], src_ap, AF.Square,
                                              accum_out=s.h[:, mu_sl + 1:mu_sl + 2]),
                  reads=(src_bb,), writes=(q, s))
            kb.op("v", lambda e: e.tensor_scalar(s.h[:, mu_sl:mu_sl + 1],
                                                 s.h[:, mu_sl:mu_sl + 1], 1.0 / D, None,
                                                 ALU.mult), reads=(s,), writes=(s,))
            kb.op("v", lambda e: e.tensor_tensor(s.h[:, mu_sl + 2:mu_sl + 3],
                                                 s.h[:, mu_sl:mu_sl + 1],
                                                 s.h[:, mu_sl:mu_sl + 1], ALU.mult),
                  reads=(s,), writes=(s,))
            kb.op("v", lambda e: e.tensor_scalar(s.h[:, mu_sl + 1:mu_sl + 2],
                                                 s.h[:, mu_sl + 1:mu_sl + 2], 1.0 / D,
                                                 s.h[:, mu_sl + 2:mu_sl + 3],
                                                 ALU.mult, ALU.subtract),
                  reads=(s,), writes=(s,))
            kb.op("s", lambda e: e.activation(s.h[:, sd_sl:sd_sl + 1],
                                              s.h[:, mu_sl + 1:mu_sl + 2], AF.Sqrt,
                                              bias=self.epst.h[:, :]),
                  reads=(s, self.epst), writes=(s,))
            kb.op("v", lambda e: e.reciprocal(s.h[:, r_sl:r_sl + 1],
                                              s.h[:, sd_sl:sd_sl + 1]),
                  reads=(s,), writes=(s,))

        for c in range(self.tch if limit is None else limit):
            i = c % 2
            own = c >= self.pch
            tok = (c - self.pch) * 128 if own else c * 128
            x, xn, xh, s, fmS = xt[i], xnt[i], xht[i], st[i], fmst[i]
            src = self.inp["xq"] if own else self.inp["xp"]
            kb.dma("qy", x, x.h[:, :], src, src.h.ap()[tok:tok + 128, :])
            xf = xnf[i]
            stats(None, x, x.h[:, :], 0, 3, 4, i)
            kb.op("v", lambda e, x=x, xf=xf, s=s: e.tensor_scalar(
                xf.h[:, :], x.h[:, :], s.h[:, 0:1], s.h[:, 3:4],
                ALU.subtract, ALU.mult), reads=(x, s), writes=(xf,))
            kb.op("s", lambda e, xf=xf, xn=xn: e.copy(xn.h[:, :], xf.h[:, :]),
                  reads=(xf,), writes=(xn,))
            stats(None, xf, xf.h[:, :], 5, 8, 9, i)
            kb.op("v", lambda e, xf=xf, xh=xh, s=s: e.tensor_scalar(
                xh.h[:, :], xf.h[:, :], s.h[:, 5:6], s.h[:, 8:9],
                ALU.subtract, ALU.mult), reads=(xf, s), writes=(xh,))
            if own and "store" in parts:
                kb.dma("qy", self.d_xn0, self.d_xn0.h.ap()[tok:tok + 128, :],
                       xn, xn.h[:, :])
            pst = self.psT[i]
            for j in range(DCH):
                kb.op("t", (lambda jj, xh=xh, pst=pst: lambda e: e.transpose(
                    pst.h[:, (jj % 8) * 128:(jj % 8) * 128 + 128],
                    xh.h[:, jj * 128:jj * 128 + 128], self.id128.h[:, :]))(j),
                    reads=(xh, self.id128), writes=(pst,))
            for j in range(DCH):
                kb.op("v", (lambda jj, pst=pst, fmS=fmS: lambda e: e.tensor_scalar(
                    fmS.h[:, jj * 128:jj * 128 + 128],
                    pst.h[:, (jj % 8) * 128:(jj % 8) * 128 + 128],
                    g1.h[:, jj:jj + 1], g1.h[:, DCH + jj:DCH + jj + 1],
                    ALU.mult, ALU.add))(j),
                    reads=(pst, g1), writes=(fmS,))
            if "ain" in parts:
                kb.dma("qg", self.d_ain,
                       self.d_ain.h.ap()[c, :, :, :].rearrange("j p c2 -> p j c2"),
                       fmS, fmS.h[:, :].rearrange("p (j c2) -> p j c2", j=DCH))
            if own and "xhat" in parts:
                oc = c - self.pch
                raw = raws[i]
                for j in range(DCH):
                    kb.op("v", (lambda jj, pst=pst, raw=raw: lambda e: e.tensor_copy(
                        raw.h[:, jj * 128:jj * 128 + 128],
                        pst.h[:, (jj % 8) * 128:(jj % 8) * 128 + 128]))(j),
                        reads=(pst,), writes=(raw,))
                kb.dma("qg", self.d_xhat,
                       self.d_xhat.h.ap()[oc, :, :, :].rearrange("j p c2 -> p j c2"),
                       raw, raw.h[:, :].rearrange("p (j c2) -> p j c2", j=DCH))
            if own and "gate" in parts:
                oc = c - self.pch
                ps = self.psB[1]
                pst32 = self.psA[0]
                for j in range(DCH):
                    kb.op("t", (lambda jj, xf=xf, pst32=pst32: lambda e: e.transpose(
                        pst32.h[:, 0:128], xf.h[:, jj * 128:jj * 128 + 128],
                        self.id128f.h[:, :]))(j),
                        reads=(xf, self.id128f), writes=(pst32,))
                    fj = fmf[i]
                    kb.op("v", (lambda fj=fj, pst32=pst32: lambda e: e.tensor_copy(
                        fj.h[:, :], pst32.h[:, 0:128]))(),
                        reads=(pst32,), writes=(fj,))
                    kb.op("t", (lambda jj, fj=fj: lambda e: e.matmul(
                        ps.h[:, 0:8], fj.h[:, :],
                        self.centTf.h[:, jj * 8:jj * 8 + 8],
                        start=(jj == 0), stop=(jj == DCH - 1)))(j),
                        reads=(fj, self.centTf), writes=(ps,))
                aff = affs[i]
                kb.op("v", lambda e, aff=aff: e.tensor_copy(
                    aff.h[:, 8:16], ps.h[:, 0:8]), reads=(ps,), writes=(aff,))
                kb.op("s", lambda e, aff=aff: e.activation(
                    aff.h[:, 16:24], aff.h[:, 8:16], AF.Sigmoid),
                    reads=(aff,), writes=(aff,))
                kb.op("v", lambda e, aff=aff: e.reduce_max(
                    aff.h[:, 24:25], aff.h[:, 8:16], axis=AX.X),
                    reads=(aff,), writes=(aff,))
                kb.op("v", lambda e, aff=aff: e.tensor_scalar(
                    aff.h[:, 32:40], aff.h[:, 8:16], aff.h[:, 24:25], None,
                    ALU.is_equal), reads=(aff,), writes=(aff,))
                kb.op("v", lambda e, aff=aff: e.tensor_tensor_scan(
                    aff.h[:, 8:16], aff.h[:, 32:40], self.zeros8.h[:, :], 0.0,
                    ALU.add, ALU.add), reads=(aff, self.zeros8), writes=(aff,))
                kb.op("v", lambda e, aff=aff: e.tensor_scalar(
                    aff.h[:, 8:16], aff.h[:, 8:16], 1.0, None, ALU.is_equal),
                    reads=(aff,), writes=(aff,))
                kb.op("v", lambda e, aff=aff: e.tensor_tensor(
                    aff.h[:, 32:40], aff.h[:, 32:40], aff.h[:, 8:16], ALU.mult),
                    reads=(aff,), writes=(aff,))
                kb.op("v", lambda e, aff=aff, oc=oc: e.tensor_tensor(
                    self.w_tm.h[:, oc * 8:oc * 8 + 8], aff.h[:, 32:40],
                    aff.h[:, 16:24], ALU.mult),
                    reads=(aff,), writes=(self.w_tm,))
        kb.barrier()
        kb.free(*xt, *xnf, *xnt, *xht, *sq, *st, *fmst, *raws, *affs, *fmf, g1)


    def emit_a2(self):
        kb = self.kb
        HP = NH // 2
        wq = kb.sb("wqkva", [128, DCH * 3 * D], BF16)
        kb.dma("qg", wq, wq.h[:, :].rearrange("p (j n) -> p j n", j=DCH),
               self.inp["w_qkv_a"],
               self.inp["w_qkv_a"].h.ap()[:, :].rearrange("(j p) n -> p j n", p=128))
        bqk = kb.sb("bqka", [128, 2 * DCH], F32)
        kb.dma("qy", bqk, bqk.h[:, :], self.inp["b_qkv_a_fm"],
               self.inp["b_qkv_a_fm"].h.ap()[:, :])
        bv = kb.sb("bva", [1, D], BF16)
        kb.dma("qg", bv, bv.h[:, :], self.inp["b_v_a"], self.inp["b_v_a"].h.ap()[:, :])
        mfirst = kb.sb("mfirst", [128, 256], F32)
        mband = kb.sb("mband", [128, 256], F32)
        kb.dma("qy", mfirst, mfirst.h[:, :], self.inp["m_swa_first"],
               self.inp["m_swa_first"].h.ap()[:, :])
        kb.dma("qy", mband, mband.h[:, :], self.inp["m_swa_band"],
               self.inp["m_swa_band"].h.ap()[:, :])
        self.swa_raw = kb.sb("swaraw", [128, HP * self.nown], BF16)

        ain = [kb.sb(f"a2ain{i}", [128, DCH * 640], BF16) for i in range(2)]
        qf = [kb.sb(f"a2q{i}", [128, HP * 512], BF16) for i in range(2)]
        kf = [kb.sb(f"a2k{i}", [128, HP * 640], BF16) for i in range(2)]
        vt = [kb.sb(f"a2v{i}", [128, 5 * 1024], BF16) for i in range(2)]
        sct = kb.sb("a2sc", [128, 256], F32)
        stt = kb.sb("a2st", [128, 4], F32)
        pet = kb.sb("a2pe", [128, 256], BF16)
        peT = kb.sb("a2peT", [128, 256], BF16)

        nst = self.nch // 4
        for st in range(nst):
            i = st % 2
            a, q, k, v = ain[i], qf[i], kf[i], vt[i]
            c0 = self.pch + st * 4 - 1
            self.load_fm("qg", a, 640, self.d_ain, c0, 5)
            for nq in range(HP):
                ps = self.psA[nq % 2]
                for j in range(DCH):
                    kb.op("t", (lambda nq=nq, j=j, a=a, ps=ps: lambda e: e.matmul(
                        ps.h[:, :], wq.h[:, j * 3072 + nq * 128: j * 3072 + nq * 128 + 128],
                        a.h[:, j * 640 + 128: (j + 1) * 640],
                        start=(j == 0), stop=(j == DCH - 1)))(),
                        reads=(wq, a), writes=(ps,))
                kb.op("v", (lambda nq=nq, q=q, ps=ps: lambda e: e.tensor_scalar(
                    q.h[:, nq * 512:(nq + 1) * 512], ps.h[:, :],
                    bqk.h[:, nq:nq + 1], 0.125, ALU.add, ALU.mult))(),
                    reads=(ps, bqk), writes=(q,))
            for nq in range(HP):
                for (s0, s1, po) in ((0, 128, 0), (128, 640, 1)):
                    ps = self.psA[2 + po]
                    for j in range(DCH):
                        kb.op("t", (lambda nq=nq, j=j, a=a, ps=ps, s0=s0, s1=s1: lambda e: e.matmul(
                            ps.h[:, 0:s1 - s0],
                            wq.h[:, j * 3072 + 1024 + nq * 128: j * 3072 + 1024 + nq * 128 + 128],
                            a.h[:, j * 640 + s0: j * 640 + s1],
                            start=(j == 0), stop=(j == DCH - 1)))(),
                            reads=(wq, a), writes=(ps,))
                    kb.op("v", (lambda nq=nq, k=k, ps=ps, s0=s0, s1=s1: lambda e: e.tensor_scalar(
                        k.h[:, nq * 640 + s0: nq * 640 + s1], ps.h[:, 0:s1 - s0],
                        bqk.h[:, DCH + nq:DCH + nq + 1], None, ALU.add))(),
                        reads=(ps, bqk), writes=(k,))
            for tc in range(5):
                for half in range(2):
                    ps = self.psA[half]
                    kb.op("t", (lambda tc=tc, half=half, ps=ps: lambda e: e.matmul(
                        ps.h[:, :], self.ones_r.h[0:1, 0:128],
                        bv.h[0:1, half * 512:(half + 1) * 512],
                        start=True, stop=False))(),
                        reads=(self.ones_r, bv), writes=(ps,))
                    for j in range(DCH):
                        kb.op("t", (lambda tc=tc, half=half, j=j, a=a, ps=ps: lambda e: e.matmul(
                            ps.h[:, :], a.h[:, j * 640 + tc * 128: j * 640 + tc * 128 + 128],
                            wq.h[:, j * 3072 + 2048 + half * 512: j * 3072 + 2048 + (half + 1) * 512],
                            start=False, stop=(j == DCH - 1)))(),
                            reads=(a, wq), writes=(ps,))
                    kb.op("v", (lambda tc=tc, half=half, v=v, ps=ps: lambda e: e.tensor_copy(
                        v.h[:, tc * 1024 + half * 512: tc * 1024 + (half + 1) * 512],
                        ps.h[:, :]))(),
                        reads=(ps,), writes=(v,))
            for hp in range(HP):
                for b in range(4):
                    oc = st * 4 + b
                    msk = mfirst if oc == 0 else mband
                    pso = self.psB[1]
                    for sub in range(2):
                        h_lo = sub * 64
                        psc = self.psB[0]
                        kb.op("t", (lambda hp=hp, b=b, sub=sub, q=q, k=k, psc=psc, h_lo=h_lo: lambda e: e.matmul(
                            psc.h[:, 0:256],
                            q.h[h_lo:h_lo + 64, hp * 512 + b * 128: hp * 512 + (b + 1) * 128],
                            k.h[h_lo:h_lo + 64, hp * 640 + b * 128: hp * 640 + b * 128 + 256],
                            start=True, stop=True))(),
                            reads=(q, k), writes=(psc,))
                        kb.op("v", (lambda msk=msk, psc=psc: lambda e: e.tensor_add(
                            sct.h[:, :], psc.h[:, 0:256], msk.h[:, :]))(),
                            reads=(psc, msk), writes=(sct,))
                        kb.op("v", lambda e: e.reduce_max(
                            stt.h[:, 0:1], sct.h[:, :], axis=AX.X, negate=True),
                            reads=(sct,), writes=(stt,))
                        kb.op("s", lambda e: e.activation(
                            pet.h[:, :], sct.h[:, :], AF.Exp, bias=stt.h[:, 0:1],
                            accum_out=stt.h[:, 1:2]),
                            reads=(sct, stt), writes=(pet, stt), raw_self=False)
                        kb.op("v", lambda e: e.reciprocal(
                            stt.h[:, 2:3], stt.h[:, 1:2]), reads=(stt,), writes=(stt,))
                        kb.op("v", lambda e: e.tensor_scalar(
                            pet.h[:, :], pet.h[:, :], stt.h[:, 2:3], None, ALU.mult),
                            reads=(pet, stt), writes=(pet,))
                        pst = self.psT[i]
                        for tb in range(2):
                            kb.op("t", (lambda tb=tb, pst=pst: lambda e: e.transpose(
                                pst.h[:, tb * 128:(tb + 1) * 128],
                                pet.h[:, tb * 128:(tb + 1) * 128], self.id128.h[:, :]))(),
                                reads=(pet, self.id128), writes=(pst,))
                        kb.op("v", (lambda pst=pst: lambda e: e.tensor_copy(
                            peT.h[:, :], pst.h[:, 0:256]))(),
                            reads=(pst,), writes=(peT,))
                        hcol = hp * 2 + sub
                        for tb in range(2):
                            kb.op("t", (lambda tb=tb, b=b, hcol=hcol, v=v, pso=pso, h_lo=h_lo: lambda e: e.matmul(
                                pso.h[h_lo:h_lo + 64, 0:128],
                                v.h[:, (b + tb) * 1024 + hcol * 64: (b + tb) * 1024 + hcol * 64 + 64],
                                peT.h[:, tb * 128:(tb + 1) * 128],
                                start=(tb == 0), stop=(tb == 1)))(),
                                reads=(v, peT), writes=(pso,))
                    kb.op("v", (lambda hp=hp, oc=oc, pso=pso: lambda e: e.tensor_copy(
                        self.swa_raw.h[:, hp * self.nown + oc * 128: hp * self.nown + oc * 128 + 128],
                        pso.h[:, 0:128]))(),
                        reads=(pso,), writes=(self.swa_raw,))
        for oc in range(self.nch):
            kb.dma("qg", self.d_swa,
                   self.d_swa.h.ap()[oc, :, :, :].rearrange("j p c -> p j c"),
                   self.swa_raw,
                   self.swa_raw.h[:, :].rearrange("p (j t) -> p j t", j=HP)[:, :, oc * 128:oc * 128 + 128])
        kb.barrier()
        kb.free(wq, bqk, bv, mfirst, mband, self.swa_raw, *ain, *qf, *kf, *vt,
                sct, stt, pet, peT)


    def emit_a3(self):
        kb = self.kb
        HP = NH // 2
        wqs = kb.sb("wqkvs", [128, DCH * 3 * D], BF16)
        kb.dma("qg", wqs, wqs.h[:, :].rearrange("p (j n) -> p j n", j=DCH),
               self.inp["w_qkv_s"],
               self.inp["w_qkv_s"].h.ap()[:, :].rearrange("(j p) n -> p j n", p=128))
        bqk = kb.sb("bqks", [128, 2 * DCH], F32)
        kb.dma("qy", bqk, bqk.h[:, :], self.inp["b_qkv_s_fm"],
               self.inp["b_qkv_s_fm"].h.ap()[:, :])
        bv = kb.sb("bvs", [1, D], BF16)
        kb.dma("qg", bv, bv.h[:, :], self.inp["b_v_s"], self.inp["b_v_s"].h.ap()[:, :])
        triu = kb.sb("triu", [128, 128], BF16)
        kb.dma("qg", triu, triu.h[:, :], self.inp["m_tril"],
               self.inp["m_tril"].h.ap()[:, :])
        onesc = kb.sb("onesc", [128, 1], BF16)
        kb.op("v", lambda e: e.memset(onesc.h[:, :], 1.0), writes=(onesc,))
        self.lin_raw = kb.sb("linraw", [128, HP * self.nown], BF16)
        S_sb = kb.sb("S_sb", [128, (NH // 2) * 65], F32)
        S_bf = kb.sb("S_bf", [128, (NH // 2) * 65], BF16)
        kb.op("v", lambda e: e.memset(S_sb.h[:, :], 0.0), writes=(S_sb,))

        ain = [kb.sb(f"a3ain{i}", [128, DCH * 512], BF16) for i in range(2)]
        qf = [kb.sb(f"a3q{i}", [128, HP * 512], BF16) for i in range(2)]
        kf = [kb.sb(f"a3k{i}", [128, HP * 512], BF16) for i in range(2)]
        vt = [kb.sb(f"a3v{i}", [128, 4 * 1024], BF16) for i in range(2)]
        ktm = [kb.sb(f"a3kt{i}", [128, 4 * 1024], BF16) for i in range(2)]
        tmp1 = kb.sb("a3t1", [128, HP * 512], BF16)
        tmp2 = kb.sb("a3t2", [128, HP * 512], BF16)
        at_s = kb.sb("a3at", [128, 128], BF16)
        dn = kb.sb("a3dn", [128, 4], F32)
        otm = kb.sb("a3otm", [128, 128], BF16)

        def elu_feat(dst_bb, dst, src_bb, src):
            kb.op("v", lambda e: e.tensor_scalar_min(tmp1.h[:, :], src, 0.0),
                  reads=(src_bb,), writes=(tmp1,))
            kb.op("s", lambda e: e.activation(tmp2.h[:, :], tmp1.h[:, :], AF.Exp),
                  reads=(tmp1,), writes=(tmp2,))
            kb.op("v", lambda e: e.tensor_scalar_max(tmp1.h[:, :], src, 0.0),
                  reads=(src_bb,), writes=(tmp1,))
            kb.op("v", lambda e: e.scalar_tensor_tensor(
                dst, tmp2.h[:, :], 0.0001, tmp1.h[:, :], ALU.add, ALU.add),
                reads=(tmp2, tmp1), writes=(dst_bb,))

        nst = self.tch // 4
        for st in range(nst):
            i = st % 2
            own = st >= self.pch // 4
            a, q, k, v, kt = ain[i], qf[i], kf[i], vt[i], ktm[i]
            c0 = st * 4
            self.load_fm("qg", a, 512, self.d_ain, c0, 4)
            if own:
                for nq in range(HP):
                    ps = self.psA[nq % 2]
                    for j in range(DCH):
                        kb.op("t", (lambda nq=nq, j=j, a=a, ps=ps: lambda e: e.matmul(
                            ps.h[:, :], wqs.h[:, j * 3072 + nq * 128: j * 3072 + nq * 128 + 128],
                            a.h[:, j * 512: (j + 1) * 512],
                            start=(j == 0), stop=(j == DCH - 1)))(),
                            reads=(wqs, a), writes=(ps,))
                    kb.op("v", (lambda nq=nq, q=q, ps=ps: lambda e: e.tensor_scalar(
                        q.h[:, nq * 512:(nq + 1) * 512], ps.h[:, :],
                        bqk.h[:, nq:nq + 1], 0.125, ALU.add, ALU.mult))(),
                        reads=(ps, bqk), writes=(q,))
                elu_feat(q, q.h[:, :], q, q.h[:, :])
            for nq in range(HP):
                ps = self.psA[2 + nq % 2]
                for j in range(DCH):
                    kb.op("t", (lambda nq=nq, j=j, a=a, ps=ps: lambda e: e.matmul(
                        ps.h[:, :], wqs.h[:, j * 3072 + 1024 + nq * 128: j * 3072 + 1024 + nq * 128 + 128],
                        a.h[:, j * 512: (j + 1) * 512],
                        start=(j == 0), stop=(j == DCH - 1)))(),
                        reads=(wqs, a), writes=(ps,))
                kb.op("v", (lambda nq=nq, k=k, ps=ps: lambda e: e.tensor_scalar(
                    k.h[:, nq * 512:(nq + 1) * 512], ps.h[:, :],
                    bqk.h[:, DCH + nq:DCH + nq + 1], None, ALU.add))(),
                    reads=(ps, bqk), writes=(k,))
            elu_feat(k, k.h[:, :], k, k.h[:, :])
            for tc in range(4):
                for half in range(2):
                    ps = self.psA[half]
                    kb.op("t", (lambda half=half, ps=ps: lambda e: e.matmul(
                        ps.h[:, :], self.ones_r.h[0:1, 0:128],
                        bv.h[0:1, half * 512:(half + 1) * 512],
                        start=True, stop=False))(),
                        reads=(self.ones_r, bv), writes=(ps,))
                    for j in range(DCH):
                        kb.op("t", (lambda tc=tc, half=half, j=j, a=a, ps=ps: lambda e: e.matmul(
                            ps.h[:, :], a.h[:, j * 512 + tc * 128: j * 512 + tc * 128 + 128],
                            wqs.h[:, j * 3072 + 2048 + half * 512: j * 3072 + 2048 + (half + 1) * 512],
                            start=False, stop=(j == DCH - 1)))(),
                            reads=(a, wqs), writes=(ps,))
                    kb.op("v", (lambda tc=tc, half=half, v=v, ps=ps: lambda e: e.tensor_copy(
                        v.h[:, tc * 1024 + half * 512: tc * 1024 + (half + 1) * 512],
                        ps.h[:, :]))(),
                        reads=(ps,), writes=(v,))
                pst = self.psT[i]
                for hp in range(HP):
                    kb.op("t", (lambda tc=tc, hp=hp, k=k, pst=pst: lambda e: e.transpose(
                        pst.h[:, hp * 128:(hp + 1) * 128],
                        k.h[:, hp * 512 + tc * 128: hp * 512 + tc * 128 + 128],
                        self.id128.h[:, :]))(),
                        reads=(k, self.id128), writes=(pst,))
                kb.op("v", (lambda tc=tc, kt=kt, pst=pst: lambda e: e.tensor_copy(
                    kt.h[:, tc * 1024:(tc + 1) * 1024], pst.h[:, :]))(),
                    reads=(pst,), writes=(kt,))

            if not own:
                for h in range(NH):
                    hp, h_lo = h // 2, (h % 2) * 64
                    psS = self.psB[1]
                    for tc in range(4):
                        kb.op("t", (lambda h=h, tc=tc, kt=kt, v=v, psS=psS, h_lo=h_lo: lambda e: e.matmul(
                            psS.h[h_lo:h_lo + 64, 0:64], kt.h[:, tc * 1024 + h * 64: tc * 1024 + h * 64 + 64],
                            v.h[:, tc * 1024 + h * 64: tc * 1024 + h * 64 + 64],
                            start=(tc == 0), stop=(tc == 3)))(),
                            reads=(kt, v), writes=(psS,))
                        kb.op("t", (lambda h=h, tc=tc, kt=kt, psS=psS, h_lo=h_lo: lambda e: e.matmul(
                            psS.h[h_lo:h_lo + 64, 64:65], kt.h[:, tc * 1024 + h * 64: tc * 1024 + h * 64 + 64],
                            onesc.h[:, :],
                            start=False, stop=(tc == 3)))(),
                            reads=(kt, onesc), writes=(psS,))
                    kb.op("v", (lambda hp=hp, h_lo=h_lo, psS=psS: lambda e: e.tensor_add(
                        S_sb.h[h_lo:h_lo + 64, hp * 65:(hp + 1) * 65],
                        S_sb.h[h_lo:h_lo + 64, hp * 65:(hp + 1) * 65],
                        psS.h[h_lo:h_lo + 64, 0:65]))(),
                        reads=(psS, S_sb), writes=(S_sb,))
                if st == self.pch // 4 - 1:
                    kb.op("v", lambda e: e.tensor_scalar(
                        S_sb.h[:, :], S_sb.h[:, :], self.hflag_t.h[:, 0:1], None,
                        ALU.mult), reads=(S_sb, self.hflag_t), writes=(S_sb,))
                continue

            for tc in range(4):
                oc = (st - self.pch // 4) * 4 + tc
                kb.op("v", lambda e: e.tensor_copy(S_bf.h[:, :], S_sb.h[:, :]),
                      reads=(S_sb,), writes=(S_bf,))
                for hp in range(HP):
                    pso = self.psT[(i + 1) % 2]
                    hp2 = hp
                    for sub in range(2):
                        h = hp * 2 + sub
                        h_lo = sub * 64
                        psc = self.psB[0]
                        kb.op("t", (lambda h=h, tc=tc, hp=hp, sub=sub, q=q, k=k, psc=psc, h_lo=h_lo: lambda e: e.matmul(
                            psc.h[:, 0:128],
                            k.h[h_lo:h_lo + 64, hp * 512 + tc * 128: hp * 512 + tc * 128 + 128],
                            q.h[h_lo:h_lo + 64, hp * 512 + tc * 128: hp * 512 + tc * 128 + 128],
                            start=True, stop=True))(),
                            reads=(q, k), writes=(psc,))
                        kb.op("v", (lambda psc=psc: lambda e: e.tensor_tensor(
                            at_s.h[:, :], psc.h[:, 0:128], triu.h[:, :], ALU.mult))(),
                            reads=(psc, triu), writes=(at_s,))
                        psn = self.psB[1]
                        kb.op("t", (lambda h=h, tc=tc, hp=hp, q=q, psn=psn, h_lo=h_lo: lambda e: e.matmul(
                            psn.h[:, 0:65],
                            q.h[h_lo:h_lo + 64, hp * 512 + tc * 128: hp * 512 + tc * 128 + 128],
                            S_bf.h[h_lo:h_lo + 64, hp * 65:(hp + 1) * 65],
                            start=True, stop=False))(),
                            reads=(q, S_bf), writes=(psn,))
                        kb.op("t", (lambda h=h, tc=tc, v=v, psn=psn: lambda e: e.matmul(
                            psn.h[:, 0:64], at_s.h[:, :],
                            v.h[:, tc * 1024 + h * 64: tc * 1024 + h * 64 + 64],
                            start=False, stop=False))(),
                            reads=(at_s, v), writes=(psn,))
                        kb.op("t", (lambda psn=psn: lambda e: e.matmul(
                            psn.h[:, 64:65], at_s.h[:, :], onesc.h[:, :],
                            start=False, stop=True))(),
                            reads=(at_s, onesc), writes=(psn,))
                        kb.op("v", (lambda psn=psn: lambda e: e.tensor_scalar_add(
                            dn.h[:, 0:1], psn.h[:, 64:65], 1e-6))(),
                            reads=(psn,), writes=(dn,))
                        kb.op("v", lambda e: e.reciprocal(dn.h[:, 1:2], dn.h[:, 0:1]),
                              reads=(dn,), writes=(dn,))
                        kb.op("v", (lambda psn=psn, h_lo=h_lo: lambda e: e.tensor_scalar(
                            otm.h[:, h_lo:h_lo + 64], psn.h[:, 0:64],
                            dn.h[:, 1:2], None, ALU.mult))(),
                            reads=(psn, dn), writes=(otm,))
                        psS = self.psB[0]
                        kb.op("t", (lambda h=h, tc=tc, kt=kt, v=v, psS=psS, h_lo=h_lo: lambda e: e.matmul(
                            psS.h[h_lo:h_lo + 64, 256:320], kt.h[:, tc * 1024 + h * 64: tc * 1024 + h * 64 + 64],
                            v.h[:, tc * 1024 + h * 64: tc * 1024 + h * 64 + 64],
                            start=True, stop=True))(),
                            reads=(kt, v), writes=(psS,))
                        kb.op("t", (lambda h=h, tc=tc, kt=kt, psS=psS, h_lo=h_lo: lambda e: e.matmul(
                            psS.h[h_lo:h_lo + 64, 320:321], kt.h[:, tc * 1024 + h * 64: tc * 1024 + h * 64 + 64],
                            onesc.h[:, :], start=False, stop=True))(),
                            reads=(kt, onesc), writes=(psS,))
                        kb.op("v", (lambda hp=hp2, h_lo=h_lo, psS=psS: lambda e: e.tensor_add(
                            S_sb.h[h_lo:h_lo + 64, hp * 65:(hp + 1) * 65],
                            S_sb.h[h_lo:h_lo + 64, hp * 65:(hp + 1) * 65],
                            psS.h[h_lo:h_lo + 64, 256:321]))(),
                            reads=(psS, S_sb), writes=(S_sb,))
                    kb.op("t", (lambda pso=pso: lambda e: e.transpose(
                        pso.h[:, 0:128], otm.h[:, :], self.id128.h[:, :]))(),
                        reads=(otm, self.id128), writes=(pso,))
                    kb.op("v", (lambda hp=hp, oc=oc, pso=pso: lambda e: e.tensor_copy(
                        self.lin_raw.h[:, hp * self.nown + oc * 128: hp * self.nown + oc * 128 + 128],
                        pso.h[:, 0:128]))(),
                        reads=(pso,), writes=(self.lin_raw,))
        for oc in range(self.nch):
            kb.dma("qg", self.d_lin,
                   self.d_lin.h.ap()[oc, :, :, :].rearrange("j p c -> p j c"),
                   self.lin_raw,
                   self.lin_raw.h[:, :].rearrange("p (j t) -> p j t", j=HP)[:, :, oc * 128:oc * 128 + 128])
        kb.barrier()
        kb.free(wqs, bqk, bv, self.lin_raw, S_sb, S_bf, *ain, *qf, *kf, *vt, *ktm,
                tmp1, tmp2, at_s, dn, otm, triu, onesc)


    def emit_a4(self):
        kb = self.kb
        wp = kb.sb("wpa", [128, DCH * D], BF16)
        kb.dma("qg", wp, wp.h[:, :].rearrange("p (j n) -> p j n", j=DCH),
               self.inp["w_p_a"],
               self.inp["w_p_a"].h.ap()[:, :].rearrange("(j p) n -> p j n", p=128))
        wos = kb.sb("wosa", [128, DCH * D], BF16)
        kb.dma("qg", wos, wos.h[:, :].rearrange("p (j n) -> p j n", j=DCH),
               self.inp["w_o_s"],
               self.inp["w_o_s"].h.ap()[:, :].rearrange("(j p) n -> p j n", p=128))
        wo = kb.sb("woa", [128, DCH * D], BF16)
        kb.dma("qg", wo, wo.h[:, :].rearrange("p (j n) -> p j n", j=DCH),
               self.inp["w_o"],
               self.inp["w_o"].h.ap()[:, :].rearrange("(j p) n -> p j n", p=128))
        bo = kb.sb("boa", [1, D], BF16)
        kb.dma("qg", bo, bo.h[:, :], self.inp["b_o"], self.inp["b_o"].h.ap()[:, :])
        bpa = kb.sb("bpa", [128, DCH], F32)
        kb.dma("qy", bpa, bpa.h[:, :], self.inp["b_p_a_fm"],
               self.inp["b_p_a_fm"].h.ap()[:, :])
        bos = kb.sb("bosf", [128, DCH], F32)
        kb.dma("qy", bos, bos.h[:, :], self.inp["b_o_s_fm"],
               self.inp["b_o_s_fm"].h.ap()[:, :])
        bpo = kb.sb("bpo", [128, DCH], F32)
        kb.op("v", lambda e: e.tensor_add(bpo.h[:, :], bpa.h[:, :], bos.h[:, :]),
              reads=(bpa, bos), writes=(bpo,))

        swa = [kb.sb(f"a4sw{i}", [128, DCH * 512], BF16) for i in range(2)]
        lin = [kb.sb(f"a4li{i}", [128, DCH * 512], BF16) for i in range(2)]
        ab = [kb.sb(f"a4ab{i}", [128, DCH * 512], BF16) for i in range(2)]
        xn = [kb.sb(f"a4xn{i}", [128, 512 // 128 * D], BF16) for i in range(2)]
        x1t = [kb.sb(f"a4x1{i}", [128, 512 // 128 * D], BF16) for i in range(2)]

        for tt in range(self.nch // 4):
            i = tt % 2
            sw, li, abf = swa[i], lin[i], ab[i]
            c0 = tt * 4
            self.load_fm("qg", sw, 512, self.d_swa, c0, 4)
            self.load_fm("qg", li, 512, self.d_lin, c0, 4)
            kb.dma("qg", xn[i], xn[i].h[:, :].rearrange("p (tc d) -> p tc d", tc=4),
                   self.d_xn0, self.d_xn0.h.ap()[c0 * 128:(c0 + 4) * 128, :]
                   .rearrange("(tc p) d -> p tc d", p=128))
            for dc in range(DCH):
                ps = self.psA[dc % 2]
                for j in range(DCH):
                    kb.op("t", (lambda dc=dc, j=j, sw=sw, ps=ps: lambda e: e.matmul(
                        ps.h[:, :], wp.h[:, j * D + dc * 128: j * D + dc * 128 + 128],
                        sw.h[:, j * 512:(j + 1) * 512],
                        start=(j == 0), stop=False))(),
                        reads=(wp, sw), writes=(ps,))
                for j in range(DCH):
                    kb.op("t", (lambda dc=dc, j=j, li=li, ps=ps: lambda e: e.matmul(
                        ps.h[:, :], wos.h[:, j * D + dc * 128: j * D + dc * 128 + 128],
                        li.h[:, j * 512:(j + 1) * 512],
                        start=False, stop=(j == DCH - 1)))(),
                        reads=(wos, li), writes=(ps,))
                kb.op("v", (lambda dc=dc, abf=abf, ps=ps: lambda e: e.tensor_scalar(
                    abf.h[:, dc * 512:(dc + 1) * 512], ps.h[:, :],
                    bpo.h[:, dc:dc + 1], None, ALU.add))(),
                    reads=(ps, bpo), writes=(abf,))
            for ts in range(4):
                for dt in range(2):
                    ps = self.psA[2 + dt]
                    kb.op("t", (lambda dt=dt, ps=ps: lambda e: e.matmul(
                        ps.h[:, :], self.ones_r.h[0:1, 0:128],
                        bo.h[0:1, dt * 512:(dt + 1) * 512],
                        start=True, stop=False))(),
                        reads=(self.ones_r, bo), writes=(ps,))
                    for j in range(DCH):
                        kb.op("t", (lambda ts=ts, dt=dt, j=j, abf=abf, ps=ps: lambda e: e.matmul(
                            ps.h[:, :], abf.h[:, j * 512 + ts * 128: j * 512 + ts * 128 + 128],
                            wo.h[:, j * D + dt * 512: j * D + (dt + 1) * 512],
                            start=False, stop=(j == DCH - 1)))(),
                            reads=(abf, wo), writes=(ps,))
                    kb.op("v", (lambda ts=ts, dt=dt, i=i, ps=ps: lambda e: e.tensor_add(
                        x1t[i].h[:, ts * D + dt * 512: ts * D + (dt + 1) * 512],
                        ps.h[:, :],
                        xn[i].h[:, ts * D + dt * 512: ts * D + (dt + 1) * 512]))(),
                        reads=(ps, xn[i]), writes=(x1t[i],))
            kb.dma("qg", self.d_x1,
                   self.d_x1.h.ap()[c0 * 128:(c0 + 4) * 128, :]
                   .rearrange("(tc p) d -> p tc d", p=128),
                   x1t[i], x1t[i].h[:, :].rearrange("p (tc d) -> p tc d", tc=4))
        kb.barrier()
        kb.free(wp, wos, wo, bo, bpa, bos, bpo, *swa, *lin, *ab, *xn, *x1t)

    def emit_a5(self):
        kb = self.kb
        w1 = kb.sb("w1a", [128, DCH * 4 * D], BF16)
        kb.dma("qg", w1, w1.h[:, :].rearrange("p (j n) -> p j n", j=DCH),
               self.inp["w_1"],
               self.inp["w_1"].h.ap()[:, :].rearrange("(j p) n -> p j n", p=128))
        w2 = kb.sb("w2a", [128, 32 * D], BF16)
        kb.dma("qg", w2, w2.h[:, :].rearrange("p (j n) -> p j n", j=32),
               self.inp["w_2"],
               self.inp["w_2"].h.ap()[:, :].rearrange("(j p) n -> p j n", p=128))
        b1m = kb.sb("b1ma", [128, 32], F32)
        kb.dma("qy", b1m, b1m.h[:, :], self.inp["b_1m_fm"],
               self.inp["b_1m_fm"].h.ap()[:, :])
        b2m = kb.sb("b2ma", [1, D], BF16)
        kb.dma("qg", b2m, b2m.h[:, :], self.inp["b_2m"], self.inp["b_2m"].h.ap()[:, :])
        gm = kb.sb("gmt", [128, 2 * DCH], F32)
        kb.dma("qy", gm, gm.h[:, :], self.inp["w_gmbm"], self.inp["w_gmbm"].h.ap()[:, :])

        x1 = [kb.sb("a5x", [128, 4 * D], BF16)] * 2
        h0f = [kb.sb("a5h", [128, DCH * 512], BF16)] * 2
        ut = [kb.sb("a5u", [128, 32 * 512], BF16)] * 2
        stat = kb.sb("a5st", [128, 8], F32)
        sqt = kb.sb("a5sq", [128, D], F32)
        xht = kb.sb("a5xh", [128, D], BF16)
        stile = kb.sb("a5s", [128, 4 * D], BF16)

        for tt in range(self.nch // 4):
            i = tt % 2
            c0 = tt * 4
            x1i, hf, u = x1[i], h0f[i], ut[i]
            kb.dma("qg", x1i, x1i.h[:, :].rearrange("p (tc d) -> p tc d", tc=4),
                   self.d_x1, self.d_x1.h.ap()[c0 * 128:(c0 + 4) * 128, :]
                   .rearrange("(tc p) d -> p tc d", p=128))
            for ts in range(4):
                sl = x1i.h[:, ts * D:(ts + 1) * D]
                kb.op("v", (lambda sl=sl, x1i=x1i: lambda e: e.reduce_sum(
                    stat.h[:, 0:1], sl, axis=AX.X))(), reads=(x1i,), writes=(stat,))
                kb.op("s", (lambda sl=sl, x1i=x1i: lambda e: e.activation(
                    sqt.h[:, :], sl, AF.Square, accum_out=stat.h[:, 1:2]))(),
                    reads=(x1i,), writes=(sqt, stat))
                kb.op("v", lambda e: e.tensor_scalar(
                    stat.h[:, 0:1], stat.h[:, 0:1], 1.0 / D, None, ALU.mult),
                    reads=(stat,), writes=(stat,))
                kb.op("v", lambda e: e.tensor_tensor(
                    stat.h[:, 2:3], stat.h[:, 0:1], stat.h[:, 0:1], ALU.mult),
                    reads=(stat,), writes=(stat,))
                kb.op("v", lambda e: e.tensor_scalar(
                    stat.h[:, 1:2], stat.h[:, 1:2], 1.0 / D, stat.h[:, 2:3],
                    ALU.mult, ALU.subtract), reads=(stat,), writes=(stat,))
                kb.op("s", lambda e: e.activation(
                    stat.h[:, 3:4], stat.h[:, 1:2], AF.Sqrt, bias=self.epst.h[:, :]),
                    reads=(stat, self.epst), writes=(stat,))
                kb.op("v", lambda e: e.reciprocal(stat.h[:, 4:5], stat.h[:, 3:4]),
                      reads=(stat,), writes=(stat,))
                kb.op("v", (lambda sl=sl, x1i=x1i: lambda e: e.tensor_scalar(
                    xht.h[:, :], sl, stat.h[:, 0:1], stat.h[:, 4:5],
                    ALU.subtract, ALU.mult))(), reads=(x1i, stat), writes=(xht,))
                pst = self.psT[i]
                for j in range(DCH):
                    kb.op("t", (lambda j=j, pst=pst: lambda e: e.transpose(
                        pst.h[:, j * 128:(j + 1) * 128],
                        xht.h[:, j * 128:(j + 1) * 128], self.id128.h[:, :]))(),
                        reads=(xht, self.id128), writes=(pst,))
                for j in range(DCH):
                    kb.op("v", (lambda ts=ts, j=j, hf=hf, pst=pst: lambda e: e.tensor_scalar(
                        hf.h[:, j * 512 + ts * 128: j * 512 + ts * 128 + 128],
                        pst.h[:, j * 128:(j + 1) * 128],
                        gm.h[:, j:j + 1], gm.h[:, DCH + j:DCH + j + 1],
                        ALU.mult, ALU.add))(),
                        reads=(pst, gm), writes=(hf,))
            for fc in range(32):
                ps = self.psA[fc % 2]
                for j in range(DCH):
                    kb.op("t", (lambda fc=fc, j=j, hf=hf, ps=ps: lambda e: e.matmul(
                        ps.h[:, :], w1.h[:, j * 4096 + fc * 128: j * 4096 + fc * 128 + 128],
                        hf.h[:, j * 512:(j + 1) * 512],
                        start=(j == 0), stop=(j == DCH - 1)))(),
                        reads=(w1, hf), writes=(ps,))
                kb.op("s", (lambda fc=fc, u=u, ps=ps: lambda e: e.activation(
                    u.h[:, fc * 512:(fc + 1) * 512], ps.h[:, :], AF.Gelu,
                    bias=b1m.h[:, fc:fc + 1]))(),
                    reads=(ps, b1m), writes=(u,))
            for ts in range(4):
                for dt in range(2):
                    ps = self.psA[2 + dt]
                    kb.op("t", (lambda dt=dt, ps=ps: lambda e: e.matmul(
                        ps.h[:, :], self.ones_r.h[0:1, 0:128],
                        b2m.h[0:1, dt * 512:(dt + 1) * 512],
                        start=True, stop=False))(),
                        reads=(self.ones_r, b2m), writes=(ps,))
                    for j in range(32):
                        kb.op("t", (lambda ts=ts, dt=dt, j=j, u=u, ps=ps: lambda e: e.matmul(
                            ps.h[:, :], u.h[:, j * 512 + ts * 128: j * 512 + ts * 128 + 128],
                            w2.h[:, j * D + dt * 512: j * D + (dt + 1) * 512],
                            start=False, stop=(j == 31)))(),
                            reads=(u, w2), writes=(ps,))
                    kb.op("v", (lambda ts=ts, dt=dt, i=i, ps=ps: lambda e: e.tensor_add(
                        stile.h[:, ts * D + dt * 512: ts * D + (dt + 1) * 512],
                        ps.h[:, :],
                        x1[i].h[:, ts * D + dt * 512: ts * D + (dt + 1) * 512]))(),
                        reads=(ps, x1[i]), writes=(stile,))
            kb.dma("qg", self.d_s,
                   self.d_s.h.ap()[c0 * 128:(c0 + 4) * 128, :]
                   .rearrange("(tc p) d -> p tc d", p=128),
                   stile, stile.h[:, :].rearrange("p (tc d) -> p tc d", tc=4))
        kb.barrier()
        kb.free(w1, w2, b1m, b2m, gm, x1[0], h0f[0], ut[0], stat, sqt, xht, stile)


    def emit_a6(self):
        kb = self.kb
        nch = self.nch
        xh = kb.sb("a6xh", [128, DCH * self.nown], BF16)
        self.load_fm("qg", xh, self.nown, self.d_xhat, 0, nch)
        gacc = kb.sb("a6g", [128, nch * D], BF16)
        kb.op("v", lambda e: e.memset(gacc.h[:, :], 0.0), writes=(gacc,))
        aine = kb.sb("a6ain", [128, DCH * self.nown], BF16)
        lng = kb.sb("a6lng", [128, DCH], F32)
        lnb = kb.sb("a6lnb", [128, DCH], F32)
        b1e = kb.sb("a6b1", [128, 32], F32)
        b2e = kb.sb("a6b2", [1, D], BF16)
        w1q = kb.sb("a6w1", [128, DCH * 1024], BF16)
        w2q = kb.sb("a6w2", [128, 8 * 1024], BF16)
        hq = kb.sb("a6h", [128, 8 * self.nown], BF16)
        tmp = kb.sb("a6tmp", [128, 512], BF16)

        for e_ in range(8):
            kb.dma("qy", lng, lng.h[:, :], self.inp["w_lng"],
                   self.inp["w_lng"].h.ap()[e_, :, :])
            kb.dma("qy", lnb, lnb.h[:, :], self.inp["w_lnb"],
                   self.inp["w_lnb"].h.ap()[e_, :, :])
            kb.dma("qy", b1e, b1e.h[:, :], self.inp["b_moe1_fm"],
                   self.inp["b_moe1_fm"].h.ap()[e_, :, :])
            kb.dma("qg", b2e, b2e.h[:, :], self.inp["b_moe2"],
                   self.inp["b_moe2"].h.ap()[e_, :, :])
            for j in range(DCH):
                kb.op("v", (lambda j=j: lambda e: e.tensor_scalar(
                    aine.h[:, j * self.nown:(j + 1) * self.nown],
                    xh.h[:, j * self.nown:(j + 1) * self.nown],
                    lng.h[:, j:j + 1], lnb.h[:, j:j + 1], ALU.mult, ALU.add))(),
                    reads=(xh, lng, lnb), writes=(aine,))
            for qq in range(4):
                kb.dma("qg", w1q, w1q.h[:, :].rearrange("p (j n) -> p j n", j=DCH),
                       self.inp["w_moe1T"],
                       self.inp["w_moe1T"].h.ap()[e_, :, qq * 1024:(qq + 1) * 1024]
                       .rearrange("(j p) n -> p j n", p=128))
                kb.dma("qg", w2q, w2q.h[:, :].rearrange("p (j n) -> p j n", j=8),
                       self.inp["w_moe2T"],
                       self.inp["w_moe2T"].h.ap()[e_, qq * 1024:(qq + 1) * 1024, :]
                       .rearrange("(j p) n -> p j n", p=128))
                for fc in range(8):
                    for tt in range(self.nch // 4):
                        ps = self.psA[(fc * (self.nch // 4) + tt) % 2]
                        for j in range(DCH):
                            kb.op("t", (lambda fc=fc, tt=tt, j=j, ps=ps: lambda e: e.matmul(
                                ps.h[:, :],
                                w1q.h[:, j * 1024 + fc * 128: j * 1024 + fc * 128 + 128],
                                aine.h[:, j * self.nown + tt * 512: j * self.nown + (tt + 1) * 512],
                                start=(j == 0), stop=(j == DCH - 1)))(),
                                reads=(w1q, aine), writes=(ps,))
                        kb.op("s", (lambda fc=fc, tt=tt, qq=qq, ps=ps: lambda e: e.activation(
                            hq.h[:, fc * self.nown + tt * 512: fc * self.nown + (tt + 1) * 512],
                            ps.h[:, :], AF.Relu, bias=b1e.h[:, qq * 8 + fc: qq * 8 + fc + 1]))(),
                            reads=(ps, b1e), writes=(hq,))
                for tc in range(nch):
                    for dt in range(2):
                        ps = self.psA[2 + dt]
                        first = (qq == 0)
                        if first:
                            kb.op("t", (lambda dt=dt, ps=ps: lambda e: e.matmul(
                                ps.h[:, :], self.ones_r.h[0:1, 0:128],
                                b2e.h[0:1, dt * 512:(dt + 1) * 512],
                                start=True, stop=False))(),
                                reads=(self.ones_r, b2e), writes=(ps,))
                        for fc in range(8):
                            kb.op("t", (lambda tc=tc, dt=dt, fc=fc, ps=ps, first=first: lambda e: e.matmul(
                                ps.h[:, :],
                                hq.h[:, fc * self.nown + tc * 128: fc * self.nown + tc * 128 + 128],
                                w2q.h[:, fc * 1024 + dt * 512: fc * 1024 + (dt + 1) * 512],
                                start=(not first and fc == 0), stop=(fc == 7)))(),
                                reads=(hq, w2q), writes=(ps,))
                        kb.op("s", (lambda tc=tc, e_=e_, ps=ps: lambda e: e.activation(
                            tmp.h[:, :], ps.h[:, :], AF.Identity,
                            scale=self.w_tm.h[:, tc * 8 + e_: tc * 8 + e_ + 1]))(),
                            reads=(ps, self.w_tm), writes=(tmp,))
                        kb.op("v", (lambda tc=tc, dt=dt: lambda e: e.tensor_add(
                            gacc.h[:, tc * D + dt * 512: tc * D + (dt + 1) * 512],
                            gacc.h[:, tc * D + dt * 512: tc * D + (dt + 1) * 512],
                            tmp.h[:, :]))(),
                            reads=(tmp, gacc), writes=(gacc,))
        xn0t = kb.sb("a6xn0", [128, D], BF16)
        gout = kb.sb("a6go", [128, D], BF16)
        for tc in range(nch):
            kb.dma("qy", xn0t, xn0t.h[:, :], self.d_xn0,
                   self.d_xn0.h.ap()[tc * 128:(tc + 1) * 128, :])
            kb.op("v", (lambda tc=tc: lambda e: e.tensor_add(
                xn0t.h[:, :], xn0t.h[:, :], gacc.h[:, tc * D:(tc + 1) * D]))(),
                reads=(xn0t, gacc), writes=(xn0t,))
            kb.op("s", lambda e: e.activation(gout.h[:, :], xn0t.h[:, :], AF.Silu),
                  reads=(xn0t,), writes=(gout,))
            kb.dma("qy", self.d_g, self.d_g.h.ap()[tc * 128:(tc + 1) * 128, :],
                   gout, gout.h[:, :])
        kb.barrier()
        kb.free(xh, gacc, aine, lng, lnb, b1e, b2e, w1q, w2q, hq, tmp, xn0t, gout)

    def emit_a7(self):
        kb = self.kb
        xt = kb.sb("a7x", [128, D], F32)
        st_ = kb.sb("a7s", [128, D], BF16)
        gt = kb.sb("a7g", [128, D], BF16)
        sg = kb.sb("a7sg", [128, D], F32)
        for tc in range(self.nch):
            kb.dma("qy", xt, xt.h[:, :], self.inp["xq"],
                   self.inp["xq"].h.ap()[tc * 128:(tc + 1) * 128, :])
            kb.dma("qy", st_, st_.h[:, :], self.d_s,
                   self.d_s.h.ap()[tc * 128:(tc + 1) * 128, :])
            kb.dma("qy", gt, gt.h[:, :], self.d_g,
                   self.d_g.h.ap()[tc * 128:(tc + 1) * 128, :])
            kb.op("v", lambda e: e.tensor_mul(sg.h[:, :], st_.h[:, :], gt.h[:, :]),
                  reads=(st_, gt), writes=(sg,))
            kb.op("v", lambda e: e.tensor_add(sg.h[:, :], sg.h[:, :], xt.h[:, :]),
                  reads=(sg, xt), writes=(sg,))
            kb.dma("qy", self.out_xnew,
                   self.out_xnew.h.ap()[tc * 128:(tc + 1) * 128, :], sg, sg.h[:, :])



def build_l0(nown=2048, npfx=2048, phases=("a1",), dbg=()):
    nc = bass.Bass("TRN2", target_bir_lowering=False)
    kb = KB(nc)
    l0 = L0(kb, nown, npfx)
    for name in dbg:
        b = getattr(l0, name)
        shp = list(b.h.shape)
        dt = b.h.dtype
        ob = kb.dram("dbg_" + name, shp, dt, kind="ExternalOutput")
        setattr(l0, name + "_dbg", ob)
    l0.emit_init()
    if "a1" in phases:
        l0.emit_a1()
    if "a2" in phases:
        l0.emit_a2()
    if "a3" in phases:
        l0.emit_a3()
    if "a4" in phases:
        l0.emit_a4()
    if "a5" in phases:
        l0.emit_a5()
    if "a6" in phases:
        l0.emit_a6()
    if "a7" in phases:
        l0.emit_a7()
    for name in dbg:
        b = getattr(l0, name)
        ob = getattr(l0, name + "_dbg")
        kb.dma("qg", ob, ob.h.ap()[...], b, b.h.ap()[...])
    kb.finalize()
    return nc, l0


def _fmvec(v, nch=DCH):
    v = np.asarray(v, np.float32)
    return np.ascontiguousarray(v.reshape(nch, 128).T)


def _row(v):
    return np.ascontiguousarray(np.asarray(v, np.float32).reshape(1, -1))


def _swa_masks():
    r = np.arange(WSWA)[:, None]
    s = np.arange(2 * WSWA)[None, :]
    band = (s >= r + 1) & (s <= r + WSWA)
    first = band & (s >= WSWA)
    mk = lambda m: np.where(m, 0.0, -1e30).astype(np.float32)
    return mk(first), mk(band)


def l0_core_inputs(I, b, h, nown=2048, npfx=2048):
    x = np.asarray(I["L0_attn_x"])
    own = np.ascontiguousarray(x[b, h * 2048: h * 2048 + nown])
    if h == 0 or npfx == 0:
        pfx = np.zeros((max(npfx, 128), D), np.float32)
    else:
        pfx = np.ascontiguousarray(x[b, 0:npfx])
    mfirst, mband = _swa_masks()
    tril = np.tril(np.ones((128, 128), np.float32))
    triu = np.ascontiguousarray(tril.T)
    d = {
        "xq": own, "xp": pfx,
        "c_id128": np.eye(128, dtype=np.float32),
        "hflag": np.full((128, 1), float(h), np.float32),
        "w_g1b1": np.concatenate([_fmvec(I["L0_attn_g1"]), _fmvec(I["L0_attn_b1"])], 1),
        "w_centT": np.ascontiguousarray(np.asarray(I["L0_ffn_centroids"]).T),
        "w_qkv_a": np.asarray(I["L0_attn_Wqkv_a"]),
        "w_qkv_s": np.asarray(I["L0_attn_Wqkv_s"]),
        "b_qkv_a_fm": np.concatenate([_fmvec(np.asarray(I["L0_attn_bqkv_a"])[0:D]),
                                      _fmvec(np.asarray(I["L0_attn_bqkv_a"])[D:2 * D])], 1),
        "b_qkv_s_fm": np.concatenate([_fmvec(np.asarray(I["L0_attn_bqkv_s"])[0:D]),
                                      _fmvec(np.asarray(I["L0_attn_bqkv_s"])[D:2 * D])], 1),
        "b_v_a": _row(np.asarray(I["L0_attn_bqkv_a"])[2 * D:]),
        "b_v_s": _row(np.asarray(I["L0_attn_bqkv_s"])[2 * D:]),
        "m_swa_first": mband if h == 1 else mfirst,
        "m_swa_band": mband,
        "m_tril": triu,
        "w_p_a": np.asarray(I["L0_attn_Wp_a"]),
        "w_o_s": np.asarray(I["L0_attn_Wo_s"]),
        "b_po_fm": _fmvec(np.asarray(I["L0_attn_bp_a"]) * 0.0),
        "b_p_a_fm": _fmvec(I["L0_attn_bp_a"]),
        "b_o_s_fm": _fmvec(I["L0_attn_bo_s"]),
        "w_o": np.asarray(I["L0_attn_Wo"]),
        "b_o": _row(I["L0_attn_bo"]),
        "w_gmbm": np.concatenate([_fmvec(I["L0_attn_gm"]), _fmvec(I["L0_attn_bm"])], 1),
        "w_1": np.asarray(I["L0_attn_W1"]),
        "b_1m_fm": _fmvec(np.asarray(I["L0_attn_b1m"]), 32),
        "w_2": np.asarray(I["L0_attn_W2"]),
        "b_2m": _row(I["L0_attn_b2m"]),
        "w_moe1T": np.ascontiguousarray(np.asarray(I["L0_ffn_w1"]).transpose(0, 2, 1)),
        "w_moe2T": np.ascontiguousarray(np.asarray(I["L0_ffn_w2"]).transpose(0, 2, 1)),
        "w_lng": np.stack([_fmvec(v) for v in np.asarray(I["L0_ffn_ln_g"])]),
        "w_lnb": np.stack([_fmvec(v) for v in np.asarray(I["L0_ffn_ln_b"])]),
        "b_moe1_fm": np.stack([_fmvec(v, 32) for v in np.asarray(I["L0_ffn_b1"])]),
        "b_moe2": np.stack([_row(v) for v in np.asarray(I["L0_ffn_b2"])]),
    }
    return {k: np.ascontiguousarray(v) for k, v in d.items()}


DI = 2048
DICH = DI // 128
NSTATE = 16
NHF = 8
EHF = 64
WIN = 512
HID = 2816
FCH = HID // 128


class L1:
    def __init__(self, kb, nown=2048, npfx=2048):
        self.kb = kb
        self.nown = nown
        self.npfx = npfx
        self.nch = nown // 128
        self.pch = npfx // 128
        self.ech = self.nch + 1
        self.tch = self.pch + self.ech
        inp = self.inp = {}

        def ext(name, shape, dtype=F32):
            b = kb.dram(name, shape, dtype, kind="ExternalInput")
            inp[name] = b
            return b
        self.ext = ext
        ext("yq", [nown + 128, D])
        ext("yp", [max(npfx, 128), D])
        ext("c_id128", [128, 128])
        ext("hflag", [128, 1])
        ext("rflag", [128, 1])
        ext("w_in", [D, 2 * DI])
        ext("b_in_x", [1, DI])
        ext("b_in_r_fm", [128, DICH])
        ext("w_conv_fm", [128, 4 * DICH])
        ext("b_conv_fm", [128, DICH])
        ext("w_dt", [DI, NSTATE])
        ext("b_dt", [16, 1])
        ext("w_xg", [128, DICH])
        ext("b_xg", [1, 1])
        ext("v_A", [16, 1])
        ext("w_Bm_fm", [128, DICH * NSTATE])
        ext("w_Cm_fm", [128, DICH * NSTATE])
        ext("w_BmT", [16, DI])
        ext("w_CmT", [16, DI])
        ext("w_out", [DI, D])
        ext("b_out", [1, D])
        ext("m_caus", [128, 128])
        ext("w_qkv_f", [2, D, 3 * 512])
        ext("w_wo_f", [2, 512, D])
        ext("w_lngb_f", [2, 128, 2 * DCH])
        ext("w_1a", [2, D, HID])
        ext("w_1g", [2, D, HID])
        ext("b_1a_fm", [2, 128, FCH])
        ext("b_1g_fm", [2, 128, FCH])
        ext("w_2f", [2, HID, D])
        ext("b_2f", [2, 128, DCH])
        ext("qs_row", [2, 64])
        ext("ks_row", [2, 64])
        ext("t_cosq", [3584, 512])
        ext("t_sinq", [3584, 512])
        ext("t_cosk", [3584, 512])
        ext("t_sink", [3584, 512])
        ext("m_loc0", [6, 8, 128, 512])
        ext("m_loc1", [5, 8, 128, 512])

        self.d_xn1 = kb.dram("d_xn1", [self.tch, DCH, 128, 128], BF16)
        self.d_xc = kb.dram("d_xc", [1 + npfx + nown + 256, DI], BF16)
        self.d_res = kb.dram("d_res", [self.nch, DICH, 128, 128], BF16)
        self.d_s1 = kb.dram("d_s1", [nown, D], BF16)
        self.d_z2 = kb.dram("d_z2", [nown, D], BF16)
        self.out_y = kb.dram("yout", [nown, D], F32, kind="ExternalOutput")

        self.id128 = kb.sb("id128", [128, 128], BF16)
        self.id128f = kb.sb("id128f", [128, 128], F32)
        self.ones_r = kb.sb("ones_r", [1, 512], BF16)
        self.ones_rf = kb.sb("ones_rf", [1, 128], F32)
        self.epst = kb.sb("epst", [128, 1], F32)
        self.hflag_t = kb.sb("hflag_t", [128, 1], F32)
        self.rflag_t = kb.sb("rflag_t", [128, 1], F32)
        self.psA = [kb.ps(f"psA{i}", [128, 512], F32) for i in range(4)]
        self.psB = [kb.ps(f"psB{i}", [128, 512], F32) for i in range(2)]
        self.psT = [kb.ps(f"psT{i}", [128, 1024], BF16) for i in range(2)]

    def emit_init(self):
        kb = self.kb
        kb.dma("qg", self.id128, self.id128.h[:, :], self.inp["c_id128"],
               self.inp["c_id128"].h.ap()[:, :])
        kb.dma("qy", self.id128f, self.id128f.h[:, :], self.inp["c_id128"],
               self.inp["c_id128"].h.ap()[:, :])
        kb.dma("qy", self.hflag_t, self.hflag_t.h[:, :], self.inp["hflag"],
               self.inp["hflag"].h.ap()[:, :])
        kb.dma("qy", self.rflag_t, self.rflag_t.h[:, :], self.inp["rflag"],
               self.inp["rflag"].h.ap()[:, :])
        kb.op("v", lambda e: e.memset(self.ones_r.h[:, :], 1.0), writes=(self.ones_r,))
        kb.op("v", lambda e: e.memset(self.ones_rf.h[:, :], 1.0), writes=(self.ones_rf,))
        kb.op("v", lambda e: e.memset(self.epst.h[:, :], EPS), writes=(self.epst,))
        self.onescol = kb.sb("onescol", [128, 1], BF16)
        kb.op("v", lambda e: e.memset(self.onescol.h[:, :], 1.0), writes=(self.onescol,))
        self.onescolf = kb.sb("onescolf", [128, 1], F32)
        kb.op("v", lambda e: e.memset(self.onescolf.h[:, :], 1.0), writes=(self.onescolf,))

    def emit_b1(self):
        kb = self.kb
        xt = [kb.sb(f"b1x{i}", [128, D], F32) for i in range(2)]
        sq = [kb.sb(f"b1q{i}", [128, D], F32) for i in range(2)]
        st = [kb.sb(f"b1s{i}", [128, 6], F32) for i in range(2)]
        xnb = [kb.sb(f"b1n{i}", [128, D], BF16) for i in range(2)]
        fmS = [kb.sb(f"b1f{i}", [128, DCH * 128], BF16) for i in range(2)]
        for c in range(self.tch):
            i = c % 2
            x, q, s, xn, fm = xt[i], sq[i], st[i], xnb[i], fmS[i]
            own = c >= self.pch
            tok = (c - self.pch) * 128 if own else c * 128
            src = self.inp["yq"] if own else self.inp["yp"]
            kb.dma("qy", x, x.h[:, :], src, src.h.ap()[tok:tok + 128, :])
            kb.op("v", (lambda x=x, s=s: lambda e: e.reduce_sum(
                s.h[:, 0:1], x.h[:, :], axis=AX.X))(), reads=(x,), writes=(s,))
            kb.op("s", (lambda x=x, q=q, s=s: lambda e: e.activation(
                q.h[:, :], x.h[:, :], AF.Square, accum_out=s.h[:, 1:2]))(),
                reads=(x,), writes=(q, s))
            kb.op("v", (lambda s=s: lambda e: e.tensor_scalar(
                s.h[:, 0:1], s.h[:, 0:1], 1.0 / D, None, ALU.mult))(),
                reads=(s,), writes=(s,))
            kb.op("v", (lambda s=s: lambda e: e.tensor_tensor(
                s.h[:, 2:3], s.h[:, 0:1], s.h[:, 0:1], ALU.mult))(),
                reads=(s,), writes=(s,))
            kb.op("v", (lambda s=s: lambda e: e.tensor_scalar(
                s.h[:, 1:2], s.h[:, 1:2], 1.0 / D, s.h[:, 2:3],
                ALU.mult, ALU.subtract))(), reads=(s,), writes=(s,))
            kb.op("s", (lambda s=s: lambda e: e.activation(
                s.h[:, 3:4], s.h[:, 1:2], AF.Sqrt, bias=self.epst.h[:, :]))(),
                reads=(s, self.epst), writes=(s,))
            kb.op("v", (lambda s=s: lambda e: e.reciprocal(
                s.h[:, 4:5], s.h[:, 3:4]))(), reads=(s,), writes=(s,))
            kb.op("v", (lambda x=x, xn=xn, s=s: lambda e: e.tensor_scalar(
                xn.h[:, :], x.h[:, :], s.h[:, 0:1], s.h[:, 4:5],
                ALU.subtract, ALU.mult))(), reads=(x, s), writes=(xn,))
            pst = self.psT[i]
            for j in range(DCH):
                kb.op("t", (lambda j=j, xn=xn, pst=pst: lambda e: e.transpose(
                    pst.h[:, j * 128:(j + 1) * 128],
                    xn.h[:, j * 128:(j + 1) * 128], self.id128.h[:, :]))(),
                    reads=(xn, self.id128), writes=(pst,))
            for j in range(DCH):
                kb.op("v", (lambda j=j, fm=fm, pst=pst: lambda e: e.tensor_copy(
                    fm.h[:, j * 128:(j + 1) * 128],
                    pst.h[:, j * 128:(j + 1) * 128]))(),
                    reads=(pst,), writes=(fm,))
            kb.dma("qg", self.d_xn1,
                   self.d_xn1.h.ap()[c, :, :, :].rearrange("j p c2 -> p j c2"),
                   fm, fm.h[:, :].rearrange("p (j c2) -> p j c2", j=DCH))
        kb.barrier()
        kb.free(*xt, *sq, *st, *xnb, *fmS)

    def emit_b2a(self):
        kb = self.kb
        w_in = kb.sb("b2win", [128, DCH * 2 * DI], BF16)
        kb.dma("qg", w_in, w_in.h[:, :].rearrange("p (j n) -> p j n", j=DCH),
               self.inp["w_in"],
               self.inp["w_in"].h.ap()[:, :].rearrange("(j p) n -> p j n", p=128))
        bx = kb.sb("b2bx", [1, DI], BF16)
        kb.dma("qg", bx, bx.h[:, :], self.inp["b_in_x"], self.inp["b_in_x"].h.ap()[:, :])
        br = kb.sb("b2br", [128, DICH], F32)
        kb.dma("qy", br, br.h[:, :], self.inp["b_in_r_fm"],
               self.inp["b_in_r_fm"].h.ap()[:, :])
        zt = kb.sb("b2z", [128, DI], BF16)
        kb.op("v", lambda e: e.memset(zt.h[:, :], 0.0), writes=(zt,))
        kb.dma("qy", self.d_xc, self.d_xc.h.ap()[0:1, :], zt, zt.h[0:1, :])
        ntail = self.d_xc.h.shape[0] - (1 + self.npfx + self.nown + 128)
        kb.dma("qy", self.d_xc,
               self.d_xc.h.ap()[1 + self.npfx + self.nown + 128:, :],
               zt, zt.h[0:ntail, :])

        xn1 = [kb.sb(f"b2xn{i}", [128, DCH * 128], BF16) for i in range(2)]
        xct = kb.sb("b2xct", [128, DI], BF16)
        resst = kb.sb("b2res", [128, DICH * 128], BF16)
        for c in range(self.tch):
            i = c % 2
            a = xn1[i]
            self.load_fm_l1("qg", a, 128, self.d_xn1, c, 1)
            own = c >= self.pch
            for dt in range(4):
                ps = self.psA[dt % 2]
                kb.op("t", (lambda dt=dt, ps=ps: lambda e: e.matmul(
                    ps.h[:, :], self.ones_r.h[0:1, 0:128],
                    bx.h[0:1, dt * 512:(dt + 1) * 512], start=True, stop=False))(),
                    reads=(self.ones_r, bx), writes=(ps,))
                for j in range(DCH):
                    kb.op("t", (lambda dt=dt, j=j, a=a, ps=ps: lambda e: e.matmul(
                        ps.h[:, :], a.h[:, j * 128:(j + 1) * 128],
                        w_in.h[:, j * 2 * DI + dt * 512: j * 2 * DI + (dt + 1) * 512],
                        start=False, stop=(j == DCH - 1)))(),
                        reads=(a, w_in), writes=(ps,))
                if c < self.pch:
                    flg = self.hflag_t
                elif c == self.tch - 1:
                    flg = self.rflag_t
                else:
                    flg = None
                if flg is None:
                    kb.op("v", (lambda dt=dt, ps=ps: lambda e: e.tensor_copy(
                        xct.h[:, dt * 512:(dt + 1) * 512], ps.h[:, :]))(),
                        reads=(ps,), writes=(xct,))
                else:
                    kb.op("v", (lambda dt=dt, ps=ps, flg=flg: lambda e: e.tensor_scalar(
                        xct.h[:, dt * 512:(dt + 1) * 512], ps.h[:, :],
                        flg.h[:, 0:1], None, ALU.mult))(),
                        reads=(ps, flg), writes=(xct,))
            kb.dma("qg", self.d_xc,
                   self.d_xc.h.ap()[1 + c * 128: 1 + (c + 1) * 128, :],
                   xct, xct.h[:, :])
            if own and c < self.tch - 1:
                oc = c - self.pch
                for rc in range(DICH):
                    ps = self.psA[2 + rc % 2]
                    for j in range(DCH):
                        kb.op("t", (lambda rc=rc, j=j, a=a, ps=ps: lambda e: e.matmul(
                            ps.h[:, 0:128],
                            w_in.h[:, j * 2 * DI + DI + rc * 128: j * 2 * DI + DI + rc * 128 + 128],
                            a.h[:, j * 128:(j + 1) * 128],
                            start=(j == 0), stop=(j == DCH - 1)))(),
                            reads=(w_in, a), writes=(ps,))
                    kb.op("v", (lambda rc=rc, ps=ps: lambda e: e.tensor_scalar(
                        resst.h[:, rc * 128:(rc + 1) * 128], ps.h[:, 0:128],
                        br.h[:, rc:rc + 1], None, ALU.add))(),
                        reads=(ps, br), writes=(resst,))
                kb.dma("qg", self.d_res,
                       self.d_res.h.ap()[oc, :, :, :].rearrange("j p c2 -> p j c2"),
                       resst, resst.h[:, :].rearrange("p (j c2) -> p j c2", j=DICH))
        kb.barrier()
        kb.free(w_in, bx, br, zt, *xn1, xct, resst)

    def load_fm_l1(self, q, sbuf_bb, colsj, dram_bb, c0, nch_ld, njd=DCH):
        for j in range(njd):
            self.kb.dma(q, sbuf_bb,
                        sbuf_bb.h[:, j * colsj: j * colsj + nch_ld * 128]
                        .rearrange("p (tc c) -> p tc c", tc=nch_ld),
                        dram_bb,
                        dram_bb.h.ap()[c0:c0 + nch_ld, j, :, :]
                        .rearrange("tc p c -> p tc c"))

    def emit_b2b(self):
        kb = self.kb
        dtw = kb.sb("b2dtw", [128, DICH * 16], BF16)
        kb.dma("qg", dtw, dtw.h[:, :].rearrange("p (j n) -> p j n", j=DICH),
               self.inp["w_dt"],
               self.inp["w_dt"].h.ap()[:, :].rearrange("(j p) n -> p j n", p=128))
        xg = kb.sb("b2xg", [128, DICH], BF16)
        kb.dma("qg", xg, xg.h[:, :], self.inp["w_xg"], self.inp["w_xg"].h.ap()[:, :])
        bdt = kb.sb("b2bdt", [16, 1], F32)
        kb.dma("qy", bdt, bdt.h[:, :], self.inp["b_dt"], self.inp["b_dt"].h.ap()[:, :])
        bxg = kb.sb("b2bxg", [1, 1], F32)
        kb.dma("qy", bxg, bxg.h[:, :], self.inp["b_xg"], self.inp["b_xg"].h.ap()[:, :])
        onest = kb.sb("b2one", [16, 1], F32)
        kb.op("v", lambda e: e.memset(onest.h[:, :], 1.0), writes=(onest,))
        mcaus = kb.sb("b2mc", [128, 128], F32)
        kb.dma("qy", mcaus, mcaus.h[:, :], self.inp["m_caus"],
               self.inp["m_caus"].h.ap()[:, :])
        self.inpw_conv = kb.sb("b2wc", [128, 4 * DICH], F32)
        kb.dma("qy", self.inpw_conv, self.inpw_conv.h[:, :], self.inp["w_conv_fm"],
               self.inp["w_conv_fm"].h.ap()[:, :])
        self.inpb_conv = kb.sb("b2bc", [128, DICH], F32)
        kb.dma("qy", self.inpb_conv, self.inpb_conv.h[:, :], self.inp["b_conv_fm"],
               self.inp["b_conv_fm"].h.ap()[:, :])
        outw = kb.sb("b2ow", [128, DICH * D], BF16)
        kb.dma("qg", outw, outw.h[:, :].rearrange("p (j n) -> p j n", j=DICH),
               self.inp["w_out"],
               self.inp["w_out"].h.ap()[:, :].rearrange("(j p) n -> p j n", p=128))
        bout = kb.sb("b2bo", [1, D], BF16)
        kb.dma("qg", bout, bout.h[:, :], self.inp["b_out"], self.inp["b_out"].h.ap()[:, :])

        self._sp1 = kb.sb("spt1", [16, 128], F32)
        self._sp2 = kb.sb("spt2", [16, 128], F32)

        def softplus16(dst_bb, dst_ap, src_bb, src_ap, w):
            t1, t2 = self._sp1, self._sp2
            kb.op("s", lambda e: e.activation(t1.h[0:16, 0:w], src_ap, AF.Abs),
                  reads=(src_bb,), writes=(t1,))
            kb.op("s", lambda e: e.activation(t2.h[0:16, 0:w], t1.h[0:16, 0:w],
                                              AF.Exp, scale=-1.0),
                  reads=(t1,), writes=(t2,))
            kb.op("s", lambda e: e.activation(t1.h[0:16, 0:w], t2.h[0:16, 0:w],
                                              AF.Ln, bias=onest.h[0:16, :]),
                  reads=(t2, onest), writes=(t1,))
            kb.op("v", lambda e: e.tensor_scalar_max(t2.h[0:16, 0:w], src_ap, 0.0),
                  reads=(src_bb,), writes=(t2,))
            kb.op("v", lambda e: e.tensor_add(dst_ap, t1.h[0:16, 0:w], t2.h[0:16, 0:w]),
                  reads=(t1, t2), writes=(dst_bb,))

        spA = kb.sb("b2spA", [16, 1], F32)
        vA = kb.sb("b2vA", [16, 1], F32)
        kb.dma("qy", vA, vA.h[:, :], self.inp["v_A"], self.inp["v_A"].h.ap()[:, :])
        softplus16(spA, spA.h[0:16, 0:1], vA, vA.h[0:16, 0:1], 1)
        dbT = kb.sb("b2dbT", [16, DI], F32)
        dcT = kb.sb("b2dcT", [16, DI], F32)
        kb.dma("qy", dcT, dcT.h[:, :], self.inp["w_CmT"], self.inp["w_CmT"].h.ap()[:, :])
        gfm = kb.sb("b2gfm", [128, DICH * 16], F32)
        onep = kb.sb("b2onep", [128, 1], F32)
        kb.op("v", lambda e: e.memset(onep.h[:, :], 1.0), writes=(onep,))
        bmT = kb.sb("b2bmT", [16, DI], F32)
        kb.dma("qy", bmT, bmT.h[:, :], self.inp["w_BmT"], self.inp["w_BmT"].h.ap()[:, :])
        wt1 = kb.sb("b2wt1", [16, DI], F32)
        onest16 = onest
        kb.op("s", lambda e: e.activation(wt1.h[0:16, :], bmT.h[0:16, :], AF.Abs),
              reads=(bmT,), writes=(wt1,))
        kb.op("s", lambda e: e.activation(dbT.h[0:16, :], wt1.h[0:16, :], AF.Exp,
                                          scale=-1.0), reads=(wt1,), writes=(dbT,))
        kb.op("s", lambda e: e.activation(wt1.h[0:16, :], dbT.h[0:16, :], AF.Ln,
                                          bias=onest16.h[0:16, :]),
              reads=(dbT, onest16), writes=(wt1,))
        kb.op("v", lambda e: e.tensor_scalar_max(dbT.h[0:16, :], bmT.h[0:16, :], 0.0),
              reads=(bmT,), writes=(dbT,))
        kb.op("v", lambda e: e.tensor_add(dbT.h[0:16, :], wt1.h[0:16, :],
                                          dbT.h[0:16, :]),
              reads=(wt1, dbT), writes=(dbT,))
        bmF = kb.sb("b2bmF", [128, DICH * 16], F32)
        kb.dma("qy", bmF, bmF.h[:, :].rearrange("p (j n) -> p j n", j=DICH),
               self.inp["w_Bm_fm"], self.inp["w_Bm_fm"].h.ap()[:, :]
               .rearrange("p (j n) -> p j n", j=DICH))
        g1t = kb.sb("b2g1", [128, DICH * 16], F32)
        g2t = kb.sb("b2g2", [128, DICH * 16], F32)
        kb.op("s", lambda e: e.activation(g1t.h[:, :], bmF.h[:, :], AF.Abs),
              reads=(bmF,), writes=(g1t,))
        kb.op("s", lambda e: e.activation(g2t.h[:, :], g1t.h[:, :], AF.Exp, scale=-1.0),
              reads=(g1t,), writes=(g2t,))
        kb.op("s", lambda e: e.activation(g1t.h[:, :], g2t.h[:, :], AF.Ln,
                                          bias=onep.h[:, :]),
              reads=(g2t, onep), writes=(g1t,))
        kb.op("v", lambda e: e.tensor_scalar_max(g2t.h[:, :], bmF.h[:, :], 0.0),
              reads=(bmF,), writes=(g2t,))
        kb.op("v", lambda e: e.tensor_add(gfm.h[:, :], g1t.h[:, :], g2t.h[:, :]),
              reads=(g1t, g2t), writes=(gfm,))
        cmF = kb.sb("b2cmF", [128, DICH * 16], F32)
        kb.dma("qy", cmF, cmF.h[:, :].rearrange("p (j n) -> p j n", j=DICH),
               self.inp["w_Cm_fm"], self.inp["w_Cm_fm"].h.ap()[:, :]
               .rearrange("p (j n) -> p j n", j=DICH))
        kb.op("v", lambda e: e.tensor_mul(gfm.h[:, :], gfm.h[:, :], cmF.h[:, :]),
              reads=(gfm, cmF), writes=(gfm,))
        kb.barrier()
        kb.free(cmF, g2t, g1t, bmF, wt1, bmT)

        hT = kb.sb("b2hT", [16, DI], F32)
        kb.op("v", lambda e: e.memset(hT.h[:, :], 0.0), writes=(hT,))
        z16 = kb.sb("b2z16", [16, 128], F32)
        kb.op("v", lambda e: e.memset(z16.h[:, :], 0.0), writes=(z16,))
        xfm = kb.sb("b2xfm", [128, DICH * 144], BF16)
        cv = kb.sb("b2cv", [128, DICH * 128], BF16)
        xs = kb.sb("b2xs", [16, 128], F32)
        dl = kb.sb("b2dl", [16, 128], F32)
        Lt = kb.sb("b2L", [16, 128], F32)
        At = kb.sb("b2At", [16, 4 * 128], BF16)
        aend = kb.sb("b2ae", [16, 4], F32)
        bbt = kb.sb("b2bb", [16, 128], BF16)
        b_tm = kb.sb("b2btm", [128, 16], BF16)
        Ht = kb.sb("b2Ht", [16, 4 * DI], BF16)
        Ltm = kb.sb("b2Ltm", [128, 4 * 16], F32)
        lrow = kb.sb("b2lr", [1, 128], F32)
        sgb = kb.sb("b2sgb", [128, 4 * 128], BF16)
        wt_all = kb.sb("b2wt", [128, 16 * 4 * 128], BF16)
        xct = [kb.sb(f"b2xc{i}", [128, DI], BF16) for i in range(4)]
        y_acc = kb.sb("b2y", [128, DICH * 512], F32)
        resg = kb.sb("b2rg", [128, DICH * 512], BF16)
        ofm = kb.sb("b2of", [128, DICH * 128], BF16)
        s1t = kb.sb("b2s1", [128, D], BF16)
        htmp = kb.sb("b2htmp", [16, DI], F32)

        ngrp = self.tch // 4
        for g in range(self.nch // 4 + self.pch // 4):
            own = g >= self.pch // 4
            for sub in range(4):
                c = g * 4 + sub
                for j in range(DICH):
                    kb.dma("qs", xfm,
                           xfm.h[:, j * 144: j * 144 + 144],
                           self.d_xc,
                           self.d_xc.h.ap()[c * 128: c * 128 + 144,
                                            j * 128:(j + 1) * 128],
                           transpose=True)
                for j in range(DICH):
                    kb.op("v", (lambda j=j: lambda e: e.tensor_scalar(
                        cv.h[:, j * 128:(j + 1) * 128],
                        xfm.h[:, j * 144: j * 144 + 128],
                        self.inpw_conv.h[:, 0 * DICH + j: 0 * DICH + j + 1],
                        self.inpb_conv.h[:, j:j + 1], ALU.mult, ALU.add))(),
                        reads=(xfm, self.inpw_conv, self.inpb_conv), writes=(cv,))
                    for kk in range(1, 4):
                        kb.op("v", (lambda j=j, kk=kk: lambda e: e.scalar_tensor_tensor(
                            cv.h[:, j * 128:(j + 1) * 128],
                            xfm.h[:, j * 144 + kk: j * 144 + kk + 128],
                            self.inpw_conv.h[:, kk * DICH + j: kk * DICH + j + 1],
                            cv.h[:, j * 128:(j + 1) * 128], ALU.mult, ALU.add))(),
                            reads=(xfm, self.inpw_conv, cv), writes=(cv,))
                psd = self.psB[0]
                for j in range(DICH):
                    kb.op("t", (lambda j=j: lambda e: e.matmul(
                        psd.h[0:16, 0:128], dtw.h[:, j * 16:(j + 1) * 16],
                        cv.h[:, j * 128:(j + 1) * 128],
                        start=(j == 0), stop=(j == DICH - 1)))(),
                        reads=(dtw, cv), writes=(psd,))
                kb.op("v", lambda e: e.tensor_scalar(
                    xs.h[0:16, :], psd.h[0:16, 0:128], bdt.h[0:16, 0:1], None,
                    ALU.add), reads=(psd, bdt), writes=(xs,))
                softplus16(dl, dl.h[0:16, :], xs, xs.h[0:16, :], 128)
                kb.op("v", lambda e: e.tensor_scalar(
                    dl.h[0:16, :], dl.h[0:16, :], spA.h[0:16, 0:1], None, ALU.mult),
                    reads=(dl, spA), writes=(dl,))
                kb.op("v", lambda e: e.tensor_tensor_scan(
                    Lt.h[0:16, :], dl.h[0:16, :], z16.h[0:16, :], 0.0,
                    ALU.add, ALU.add), reads=(dl, z16), writes=(Lt,))
                kb.op("s", (lambda sub=sub: lambda e: e.activation(
                    At.h[0:16, sub * 128:(sub + 1) * 128], Lt.h[0:16, :],
                    AF.Exp, scale=-1.0))(), reads=(Lt,), writes=(At,))
                kb.op("v", (lambda sub=sub: lambda e: e.tensor_copy(
                    aend.h[0:16, sub:sub + 1],
                    At.h[0:16, sub * 128 + 127: sub * 128 + 128]))(),
                    reads=(At,), writes=(aend,))
                kb.op("v", lambda e: e.tensor_scalar(
                    xs.h[0:16, :], Lt.h[0:16, :], Lt.h[0:16, 127:128], None,
                    ALU.subtract), reads=(Lt,), writes=(xs,))
                kb.op("s", lambda e: e.activation(bbt.h[0:16, :], xs.h[0:16, :],
                                                  AF.Exp),
                      reads=(xs,), writes=(bbt,))
                pstr = self.psT[0]
                kb.op("t", lambda e: e.transpose(
                    pstr.h[:, 0:16], bbt.h[0:16, :], self.id128.h[0:16, 0:16]),
                    reads=(bbt, self.id128), writes=(pstr,))
                kb.op("v", lambda e: e.tensor_copy(b_tm.h[:, :], pstr.h[:, 0:16]),
                      reads=(pstr,), writes=(b_tm,))
                kb.dma("qg", xct[sub], xct[sub].h[:, :], self.d_xc,
                       self.d_xc.h.ap()[1 + c * 128: 1 + (c + 1) * 128, :])
                if own:
                    kb.op("v", (lambda sub=sub: lambda e: e.tensor_mul(
                        Ht.h[0:16, sub * DI:(sub + 1) * DI], hT.h[0:16, :],
                        dcT.h[0:16, :]))(), reads=(hT, dcT), writes=(Ht,))
                    pstf = self.psB[1]
                    kb.op("t", lambda e: e.transpose(
                        pstf.h[:, 0:16], Lt.h[0:16, :], self.id128f.h[0:16, 0:16]),
                        reads=(Lt, self.id128f), writes=(pstf,))
                    kb.op("v", (lambda sub=sub: lambda e: e.tensor_copy(
                        Ltm.h[:, sub * 16:(sub + 1) * 16], pstf.h[:, 0:16]))(),
                        reads=(pstf,), writes=(Ltm,))
                    psg = self.psB[1]
                    for j in range(DICH):
                        kb.op("t", (lambda j=j: lambda e: e.matmul(
                            psg.h[0:1, 128:256], xg.h[:, j:j + 1],
                            cv.h[:, j * 128:(j + 1) * 128],
                            start=(j == 0), stop=(j == DICH - 1)))(),
                            reads=(xg, cv), writes=(psg,))
                    kb.op("s", lambda e: e.activation(
                        lrow.h[0:1, :], psg.h[0:1, 128:256], AF.Silu,
                        bias=bxg.h[0:1, 0:1]), reads=(psg, bxg), writes=(lrow,))
                    sgr = kb.sb("b2sgr", [1, 128], BF16) if not hasattr(self, "_sgr") else self._sgr
                    self._sgr = sgr
                    kb.op("v", lambda e: e.tensor_copy(sgr.h[0:1, :], lrow.h[0:1, :]),
                          reads=(lrow,), writes=(sgr,))
                    psb2 = self.psB[0]
                    kb.op("t", lambda e: e.matmul(
                        psb2.h[:, 384:512], self.ones_r.h[0:1, 0:128], sgr.h[0:1, :],
                        start=True, stop=True), reads=(self.ones_r, sgr), writes=(psb2,))
                    kb.op("v", (lambda sub=sub: lambda e: e.tensor_copy(
                        sgb.h[:, sub * 128:(sub + 1) * 128], psb2.h[:, 384:512]))(),
                        reads=(psb2,), writes=(sgb,))
                    for n in range(16):
                        kb.op("t", (lambda sub=sub, n=n: lambda e: e.transpose(
                            pstf.h[0:1, 128:256], Ltm.h[:, sub * 16 + n: sub * 16 + n + 1],
                            self.id128f.h[:, :]))(),
                            reads=(Ltm, self.id128f), writes=(pstf,))
                        kb.op("v", lambda e: e.tensor_copy(
                            lrow.h[0:1, :], pstf.h[0:1, 128:256]),
                            reads=(pstf,), writes=(lrow,))
                        psD = self.psB[0]
                        kb.op("t", lambda e: e.matmul(
                            psD.h[:, 128:256], self.ones_rf.h[0:1, :],
                            lrow.h[0:1, :], start=True, stop=True),
                            reads=(self.ones_rf, lrow), writes=(psD,))
                        kb.op("v", (lambda sub=sub, n=n: lambda e: e.tensor_scalar(
                            cv.h[:, 0:128],
                            psD.h[:, 128:256],
                            Ltm.h[:, sub * 16 + n: sub * 16 + n + 1], -1.0,
                            ALU.subtract, ALU.mult))(),
                            reads=(psD, Ltm), writes=(cv,))
                        kb.op("v", lambda e: e.tensor_add(
                            cv.h[:, 0:128], cv.h[:, 0:128], mcaus.h[:, :]),
                            reads=(cv, mcaus), writes=(cv,))
                        kb.op("s", (lambda sub=sub, n=n: lambda e: e.activation(
                            wt_all.h[:, (n * 4 + sub) * 128:(n * 4 + sub) * 128 + 128],
                            cv.h[:, 0:128], AF.Exp))(),
                            reads=(cv,), writes=(wt_all,))
                psS = self.psB[1]
                for dt in range(4):
                    kb.op("t", (lambda dt=dt, sub=sub: lambda e: e.matmul(
                        psS.h[0:16, 0:512], b_tm.h[:, :],
                        xct[sub].h[:, dt * 512:(dt + 1) * 512],
                        start=True, stop=True))(),
                        reads=(b_tm, xct[sub]), writes=(psS,))
                    kb.op("v", (lambda dt=dt, sub=sub: lambda e: e.tensor_scalar(
                        htmp.h[0:16, dt * 512:(dt + 1) * 512], hT.h[0:16, dt * 512:(dt + 1) * 512],
                        aend.h[0:16, sub:sub + 1], None, ALU.mult))(),
                        reads=(hT, aend), writes=(htmp,))
                    kb.op("v", (lambda dt=dt: lambda e: e.tensor_mul(
                        hT.h[0:16, dt * 512:(dt + 1) * 512], psS.h[0:16, 0:512],
                        dbT.h[0:16, dt * 512:(dt + 1) * 512]))(),
                        reads=(psS, dbT), writes=(hT,))
                    kb.op("v", (lambda dt=dt: lambda e: e.tensor_add(
                        hT.h[0:16, dt * 512:(dt + 1) * 512],
                        hT.h[0:16, dt * 512:(dt + 1) * 512],
                        htmp.h[0:16, dt * 512:(dt + 1) * 512]))(),
                        reads=(hT, htmp), writes=(hT,))
            if g == self.pch // 4 - 1:
                kb.op("v", lambda e: e.tensor_scalar(
                    hT.h[0:16, :], hT.h[0:16, :], self.hflag_t.h[0:16, 0:1], None,
                    ALU.mult), reads=(hT, self.hflag_t), writes=(hT,))
            if not own:
                continue
            self.load_fm_l1("qg", resg, 512, self.d_res,
                            (g - self.pch // 4) * 4, 4, njd=DICH)
            for wave in range(4):
                dcs = list(range(wave * 4, wave * 4 + 4))
                for sub in range(4):
                    for ii, dc in enumerate(dcs):
                        kb.op("t", (lambda sub=sub, dc=dc, ii=ii: lambda e: e.matmul(
                            self.psA[ii].h[:, sub * 128:(sub + 1) * 128],
                            Ht.h[0:16, sub * DI + dc * 128: sub * DI + (dc + 1) * 128],
                            At.h[0:16, sub * 128:(sub + 1) * 128],
                            start=True, stop=True))(),
                            reads=(Ht, At), writes=(self.psA[ii],))
                for ii, dc in enumerate(dcs):
                    kb.op("v", (lambda dc=dc, ii=ii: lambda e: e.tensor_copy(
                        y_acc.h[:, dc * 512:(dc + 1) * 512],
                        self.psA[ii].h[:, :]))(),
                        reads=(self.psA[ii],), writes=(y_acc,))
                for n in range(16):
                    for ii, dc in enumerate(dcs):
                        for sub in range(4):
                            kb.op("t", (lambda n=n, dc=dc, sub=sub, ii=ii: lambda e: e.matmul(
                                self.psA[ii].h[:, sub * 128:(sub + 1) * 128],
                                xct[sub].h[:, dc * 128:(dc + 1) * 128],
                                wt_all.h[:, (n * 4 + sub) * 128:(n * 4 + sub) * 128 + 128],
                                start=True, stop=True))(),
                                reads=(xct[sub], wt_all), writes=(self.psA[ii],))
                    for ii, dc in enumerate(dcs):
                        kb.op("v", (lambda n=n, dc=dc, ii=ii: lambda e: e.scalar_tensor_tensor(
                            y_acc.h[:, dc * 512:(dc + 1) * 512],
                            self.psA[ii].h[:, :],
                            gfm.h[:, dc * 16 + n: dc * 16 + n + 1],
                            y_acc.h[:, dc * 512:(dc + 1) * 512],
                            ALU.mult, ALU.add))(),
                            reads=(self.psA[ii], gfm, y_acc), writes=(y_acc,))
            for sub in range(4):
                oc = (g - self.pch // 4) * 4 + sub
                for dc in range(DICH):
                    kb.op("v", (lambda dc=dc, sub=sub: lambda e: e.tensor_tensor(
                        ofm.h[:, dc * 128:(dc + 1) * 128],
                        y_acc.h[:, dc * 512 + sub * 128: dc * 512 + sub * 128 + 128],
                        sgb.h[:, sub * 128:(sub + 1) * 128], ALU.mult))(),
                        reads=(y_acc, sgb), writes=(ofm,))
                    kb.op("v", (lambda dc=dc, sub=sub: lambda e: e.tensor_add(
                        ofm.h[:, dc * 128:(dc + 1) * 128],
                        ofm.h[:, dc * 128:(dc + 1) * 128],
                        resg.h[:, dc * 512 + sub * 128: dc * 512 + sub * 128 + 128]))(),
                        reads=(ofm, resg), writes=(ofm,))
                for dt in range(2):
                    ps = self.psA[2 + dt]
                    kb.op("t", (lambda dt=dt, ps=ps: lambda e: e.matmul(
                        ps.h[:, :], self.ones_r.h[0:1, 0:128],
                        bout.h[0:1, dt * 512:(dt + 1) * 512],
                        start=True, stop=False))(),
                        reads=(self.ones_r, bout), writes=(ps,))
                    for dc in range(DICH):
                        kb.op("t", (lambda dt=dt, dc=dc, ps=ps: lambda e: e.matmul(
                            ps.h[:, :], ofm.h[:, dc * 128:(dc + 1) * 128],
                            outw.h[:, dc * D + dt * 512: dc * D + (dt + 1) * 512],
                            start=False, stop=(dc == DICH - 1)))(),
                            reads=(ofm, outw), writes=(ps,))
                    kb.op("v", (lambda dt=dt, ps=ps: lambda e: e.tensor_copy(
                        s1t.h[:, dt * 512:(dt + 1) * 512], ps.h[:, :]))(),
                        reads=(ps,), writes=(s1t,))
                kb.dma("qg", self.d_s1,
                       self.d_s1.h.ap()[oc * 128:(oc + 1) * 128, :], s1t, s1t.h[:, :])
        kb.barrier()
        kb.free(dtw, xg, bdt, bxg, onest, mcaus, outw, bout, spA, vA, dbT,
                dcT, gfm, onep, hT, z16, xfm, cv, xs, dl, Lt, At,
                aend, bbt, b_tm, Ht, Ltm, lrow, sgb, wt_all, *xct, y_acc, resg,
                ofm, s1t, htmp, self._sp1, self._sp2, self._sgr,
                self.inpw_conv, self.inpb_conv)

    def emit_b3(self):
        kb = self.kb
        self.d_za = kb.dram("d_za", [25, DCH, 128, 128], BF16)
        self.d_z1 = kb.dram("d_z1", [24, DCH, 128, 128], BF16)
        self.d_z2f = kb.dram("d_z2f", [16, DCH, 128, 128], BF16)
        self.negb = kb.sb("b3neg", [128, 1], F32)
        kb.op("v", lambda e: e.memset(self.negb.h[:, :], -30.0), writes=(self.negb,))
        zz = kb.sb("b3zz", [128, DCH * 128], BF16)
        kb.op("v", lambda e: e.memset(zz.h[:, :], 0.0), writes=(zz,))
        kb.dma("qg", self.d_za, self.d_za.h.ap()[0, :, :, :].rearrange("j p c -> p j c"),
               zz, zz.h[:, :].rearrange("p (j c) -> p j c", j=DCH))
        kb.barrier()
        kb.free(zz)
        self._b3_attn(0, 28, 6, self.d_xn1, self.pch - 12, 0)
        self._b3_mlp(0, 24, self.d_za, self.d_z1, 0)
        self._b3_attn(1, 24, 5, self.d_z1, 0, 512)
        self._b3_mlp(1, 20, self.d_za, self.d_z2f, 4)

    def _b3_attn(self, lay, nkv, nq, src_d, src_base, tab0):
        kb = self.kb
        qoff = 512
        wq = kb.sb("b3wq", [128, DCH * 1536], BF16)
        kb.dma("qg", wq, wq.h[:, :].rearrange("p (j n) -> p j n", j=DCH),
               self.inp["w_qkv_f"],
               self.inp["w_qkv_f"].h.ap()[lay, :, :].rearrange("(j p) n -> p j n", p=128))
        wo = kb.sb("b3wo", [64, 8 * D], BF16)
        kb.dma("qg", wo, wo.h[:, :].rearrange("p (h n) -> p h n", h=8),
               self.inp["w_wo_f"],
               self.inp["w_wo_f"].h.ap()[lay, :, :].rearrange("(h p) n -> p h n", p=64))
        qsb = kb.sb("b3qsb", [128, 512], BF16)
        ksb = kb.sb("b3ksb", [128, 512], BF16)
        s64 = kb.sb("b3s64", [1, 512], BF16)
        for nm, dst in (("qs_row", qsb), ("ks_row", ksb)):
            kb.dma("qg", s64, s64.h[0:1, 0:64], self.inp[nm],
                   self.inp[nm].h.ap()[lay:lay + 1, :])
            for hh in range(1, 8):
                kb.op("v", (lambda hh=hh: lambda e: e.tensor_copy(
                    s64.h[0:1, hh * 64:(hh + 1) * 64], s64.h[0:1, 0:64]))(),
                    reads=(s64,), writes=(s64,))
            psq = self.psB[0]
            kb.op("t", lambda e: e.matmul(
                psq.h[:, :], self.ones_r.h[0:1, 0:128], s64.h[0:1, :],
                start=True, stop=True), reads=(self.ones_r, s64), writes=(psq,))
            kb.op("v", (lambda dst=dst: lambda e: e.tensor_copy(
                dst.h[:, :], psq.h[:, :]))(), reads=(psq,), writes=(dst,))

        qf = kb.sb("b3qf", [128, 4 * nq * 512], BF16)
        kf = kb.sb("b3kf", [128, 4 * nkv * 128], BF16)
        vx = kb.sb("b3vx", [128, nkv * 520], BF16)
        zsc = kb.sb("b3zsc", [128, DCH * 129], BF16)
        qkt = kb.sb("b3qkt", [128, 1536], F32)
        qtm = kb.sb("b3qtm", [128, 512], BF16)
        ktm = kb.sb("b3ktm", [128, 512], BF16)
        prm = kb.sb("b3prm", [128, 512], BF16)
        t1b = kb.sb("b3t1", [128, 512], BF16)
        tcs = kb.sb("b3tcs", [128, 4 * 512], BF16)
        nrm = kb.sb("b3nrm", [128, 40], F32)
        sq5 = kb.sb("b3sq5", [128, 512], F32)
        msk = kb.sb("b3msk", [128, 8 * 512], F32)
        et = kb.sb("b3et", [128, 512], BF16)
        sct = kb.sb("b3sct", [128, 512], F32)
        dnr = kb.sb("b3dnr", [1, 512], F32)
        dni = kb.sb("b3dni", [1, 512], BF16)
        osb = kb.sb("b3osb", [64, 512], BF16)
        ofh = kb.sb("b3ofh", [64, 8 * 512], BF16)
        zat = kb.sb("b3zat", [128, DCH * 128], BF16)

        for c in range(nkv):
            pc = max(src_base + c - 1, 0)
            for j in range(DCH):
                kb.dma("qg", zsc, zsc.h[:, j * 129 + 1: j * 129 + 129],
                       src_d, src_d.h.ap()[src_base + c, j, :, :])
                kb.dma("qg", zsc, zsc.h[:, j * 129: j * 129 + 1],
                       src_d, src_d.h.ap()[pc, j, :, 127:128],
                       allow_slow_non_contiguous=True)
            for ti, tn in enumerate(("t_cosq", "t_sinq", "t_cosk", "t_sink")):
                kb.dma("qg", tcs, tcs.h[:, ti * 512:(ti + 1) * 512],
                       self.inp[tn],
                       self.inp[tn].h.ap()[tab0 + c * 128: tab0 + c * 128 + 128, :])
            for part in range(3):
                ps = self.psA[part % 2]
                for j in range(DCH):
                    off = 0 if j >= DCH // 2 else 1
                    kb.op("t", (lambda part=part, j=j, off=off, ps=ps: lambda e: e.matmul(
                        ps.h[:, :], zsc.h[:, j * 129 + off: j * 129 + off + 128],
                        wq.h[:, j * 1536 + part * 512: j * 1536 + (part + 1) * 512],
                        start=(j == 0), stop=(j == DCH - 1)))(),
                        reads=(zsc, wq), writes=(ps,))
                kb.op("v", (lambda part=part, ps=ps: lambda e: e.tensor_copy(
                    qkt.h[:, part * 512:(part + 1) * 512], ps.h[:, :]))(),
                    reads=(ps,), writes=(qkt,))
            for (base, outb, scl, tbc, tbs) in ((0, qtm, qsb, 0, 1),
                                                (512, ktm, ksb, 2, 3)):
                kb.op("s", (lambda base=base: lambda e: e.activation(
                    sq5.h[:, :], qkt.h[:, base:base + 512], AF.Square))(),
                    reads=(qkt,), writes=(sq5,))
                kb.op("v", lambda e: e.reduce_sum(
                    nrm.h[:, 0:8], sq5.h[:, :].rearrange("p (h e) -> p h e", h=8),
                    axis=AX.X), reads=(sq5,), writes=(nrm,))
                kb.op("s", lambda e: e.activation(nrm.h[:, 8:16], nrm.h[:, 0:8],
                                                  AF.Sqrt),
                      reads=(nrm,), writes=(nrm,))
                kb.op("v", lambda e: e.tensor_scalar_max(
                    nrm.h[:, 8:16], nrm.h[:, 8:16], 1e-12),
                    reads=(nrm,), writes=(nrm,))
                kb.op("v", lambda e: e.reciprocal(nrm.h[:, 16:24], nrm.h[:, 8:16]),
                      reads=(nrm,), writes=(nrm,))
                for hh in range(8):
                    kb.op("v", (lambda base=base, hh=hh, outb=outb: lambda e: e.tensor_scalar(
                        outb.h[:, hh * 64:(hh + 1) * 64],
                        qkt.h[:, base + hh * 64: base + (hh + 1) * 64],
                        nrm.h[:, 16 + hh:17 + hh], None, ALU.mult))(),
                        reads=(qkt, nrm), writes=(outb,))
                kb.op("v", (lambda outb=outb, scl=scl: lambda e: e.tensor_mul(
                    outb.h[:, :], outb.h[:, :], scl.h[:, :]))(),
                    reads=(outb, scl), writes=(outb,))
                kb.op("v", (lambda outb=outb: lambda e: e.tensor_copy(
                    prm.h[:, :].rearrange("p (h v) -> p h v", h=8)[:, :, 0:32],
                    outb.h[:, :].rearrange("p (h v) -> p h v", h=8)[:, :, 32:64]))(),
                    reads=(outb,), writes=(prm,))
                kb.op("v", (lambda outb=outb: lambda e: e.tensor_copy(
                    prm.h[:, :].rearrange("p (h v) -> p h v", h=8)[:, :, 32:64],
                    outb.h[:, :].rearrange("p (h v) -> p h v", h=8)[:, :, 0:32]))(),
                    reads=(outb,), writes=(prm,))
                kb.op("v", (lambda outb=outb, tbc=tbc: lambda e: e.tensor_mul(
                    t1b.h[:, :], outb.h[:, :], tcs.h[:, tbc * 512:(tbc + 1) * 512]))(),
                    reads=(outb, tcs), writes=(t1b,))
                kb.op("v", (lambda tbs=tbs: lambda e: e.tensor_mul(
                    prm.h[:, :], prm.h[:, :], tcs.h[:, tbs * 512:(tbs + 1) * 512]))(),
                    reads=(prm, tcs), writes=(prm,))
                kb.op("v", (lambda outb=outb: lambda e: e.tensor_add(
                    outb.h[:, :], t1b.h[:, :], prm.h[:, :]))(),
                    reads=(t1b, prm), writes=(outb,))
            kb.op("v", (lambda c=c: lambda e: e.tensor_copy(
                vx.h[:, c * 520: c * 520 + 520]
                .rearrange("p (h v) -> p h v", v=65)[:, :, 0:64],
                qkt.h[:, 1024:1536].rearrange("p (h v) -> p h v", h=8)))(),
                reads=(qkt,), writes=(vx,))
            kb.op("v", (lambda c=c: lambda e: e.memset(
                vx.h[:, c * 520: c * 520 + 520]
                .rearrange("p (h v) -> p h v", v=65)[:, :, 64:65], 1.0))(),
                writes=(vx,))
            i = c % 2
            pst = self.psT[i]
            for hp in range(4):
                kb.op("t", (lambda hp=hp, pst=pst: lambda e: e.transpose(
                    pst.h[:, hp * 128:(hp + 1) * 128],
                    ktm.h[:, hp * 128:(hp + 1) * 128], self.id128.h[:, :]))(),
                    reads=(ktm, self.id128), writes=(pst,))
            for hp in range(4):
                kb.op("v", (lambda c=c, hp=hp, pst=pst: lambda e: e.tensor_copy(
                    kf.h[:, hp * nkv * 128 + c * 128: hp * nkv * 128 + (c + 1) * 128],
                    pst.h[:, hp * 128:(hp + 1) * 128]))(),
                    reads=(pst,), writes=(kf,))
            qcol = c * 128 - qoff
            if 0 <= qcol < nq * 512:
                pst2 = self.psT[(i + 1) % 2]
                for hp in range(4):
                    kb.op("t", (lambda hp=hp, pst2=pst2: lambda e: e.transpose(
                        pst2.h[:, hp * 128:(hp + 1) * 128],
                        qtm.h[:, hp * 128:(hp + 1) * 128], self.id128.h[:, :]))(),
                        reads=(qtm, self.id128), writes=(pst2,))
                for hp in range(4):
                    kb.op("v", (lambda qcol=qcol, hp=hp, pst2=pst2: lambda e: e.tensor_copy(
                        qf.h[:, hp * nq * 512 + qcol: hp * nq * 512 + qcol + 128],
                        pst2.h[:, hp * 128:(hp + 1) * 128]))(),
                        reads=(pst2,), writes=(qf,))
        for w in range(nq):
            mname = "m_loc0" if lay == 0 else "m_loc1"
            kb.dma("qy", msk, msk.h[:, :].rearrange("p (kc i2) -> p kc i2", kc=8),
                   self.inp[mname],
                   self.inp[mname].h.ap()[w, :, :, :].rearrange("kc p i2 -> p kc i2"))
            for h in range(8):
                hp, h_lo = h // 2, (h % 2) * 64
                pso = self.psB[1]
                for kc in range(8):
                    kcu = 4 * w + kc
                    psc = self.psB[0]
                    kb.op("t", (lambda hp=hp, h_lo=h_lo, kcu=kcu, w=w, psc=psc: lambda e: e.matmul(
                        psc.h[:, :],
                        kf.h[h_lo:h_lo + 64, hp * nkv * 128 + kcu * 128: hp * nkv * 128 + (kcu + 1) * 128],
                        qf.h[h_lo:h_lo + 64, hp * nq * 512 + w * 512: hp * nq * 512 + (w + 1) * 512],
                        start=True, stop=True))(),
                        reads=(kf, qf), writes=(psc,))
                    kb.op("v", (lambda kc=kc, psc=psc: lambda e: e.tensor_add(
                        sct.h[:, :], psc.h[:, :], msk.h[:, kc * 512:(kc + 1) * 512]))(),
                        reads=(psc, msk), writes=(sct,))
                    kb.op("s", lambda e: e.activation(
                        et.h[:, :], sct.h[:, :], AF.Exp, scale=8.0,
                        bias=self.negb.h[:, 0:1]),
                        reads=(sct, self.negb), writes=(et,))
                    kb.op("t", (lambda h=h, kcu=kcu, pso=pso, kc=kc: lambda e: e.matmul(
                        pso.h[0:65, 0:512],
                        vx.h[:, kcu * 520 + h * 65: kcu * 520 + (h + 1) * 65],
                        et.h[:, :], start=(kc == 0), stop=(kc == 7)))(),
                        reads=(vx, et), writes=(pso,))
                kb.op("v", (lambda pso=pso: lambda e: e.tensor_scalar_max(
                    dnr.h[0:1, :], pso.h[64:65, 0:512], 1e-38))(),
                    reads=(pso,), writes=(dnr,))
                kb.op("v", lambda e: e.reciprocal(dnr.h[0:1, :], dnr.h[0:1, :]),
                      reads=(dnr,), writes=(dnr,))
                kb.op("v", lambda e: e.tensor_copy(dni.h[0:1, :], dnr.h[0:1, :]),
                      reads=(dnr,), writes=(dni,))
                psb = self.psB[0]
                kb.op("t", lambda e: e.matmul(
                    psb.h[0:64, 0:512], self.ones_r.h[0:1, 0:64], dni.h[0:1, :],
                    start=True, stop=True), reads=(self.ones_r, dni), writes=(psb,))
                kb.op("v", (lambda pso=pso: lambda e: e.tensor_copy(
                    osb.h[0:64, :], pso.h[0:64, 0:512]))(),
                    reads=(pso,), writes=(osb,))
                kb.op("v", (lambda h=h, psb=psb: lambda e: e.tensor_tensor(
                    ofh.h[0:64, h * 512:(h + 1) * 512],
                    osb.h[0:64, :], psb.h[0:64, 0:512], ALU.mult))(),
                    reads=(osb, psb), writes=(ofh,))
            for ts in range(4):
                zc = (nkv - 4 * nq) - 4 + w * 4 + ts
                for j in range(DCH):
                    kb.dma("qg", zat, zat.h[:, j * 128:(j + 1) * 128],
                           src_d, src_d.h.ap()[src_base + zc + 4, j, :, :])
                for dc in range(DCH):
                    ps = self.psA[2 + dc % 2]
                    for h in range(8):
                        kb.op("t", (lambda dc=dc, h=h, ts=ts, ps=ps: lambda e: e.matmul(
                            ps.h[:, 0:128],
                            wo.h[0:64, h * D + dc * 128: h * D + dc * 128 + 128],
                            ofh.h[0:64, h * 512 + ts * 128: h * 512 + ts * 128 + 128],
                            start=(h == 0), stop=(h == 7)))(),
                            reads=(wo, ofh), writes=(ps,))
                    kb.op("v", (lambda dc=dc, ps=ps: lambda e: e.tensor_add(
                        zat.h[:, dc * 128:(dc + 1) * 128], ps.h[:, 0:128],
                        zat.h[:, dc * 128:(dc + 1) * 128]))(),
                        reads=(ps, zat), writes=(zat,))
                oz = w * 4 + ts + 1 + (0 if lay == 0 else 0)
                kb.dma("qg", self.d_za,
                       self.d_za.h.ap()[oz, :, :, :].rearrange("j p c2 -> p j c2"),
                       zat, zat.h[:, :].rearrange("p (j c2) -> p j c2", j=DCH))
        kb.barrier()
        kb.free(wq, wo, qsb, ksb, s64, qf, kf, vx, zsc, qkt, qtm, ktm, prm,
                t1b, tcs, nrm, sq5, msk, et, sct, dnr, dni, osb, ofh, zat)

    def _b3_mlp(self, lay, nmlp, za_d, out_d, out_skip):
        kb = self.kb
        w1a = kb.sb("b3w1a", [128, DCH * HID], BF16)
        kb.dma("qg", w1a, w1a.h[:, :].rearrange("p (j n) -> p j n", j=DCH),
               self.inp["w_1a"],
               self.inp["w_1a"].h.ap()[lay, :, :].rearrange("(j p) n -> p j n", p=128))
        w1g = kb.sb("b3w1g", [128, DCH * HID], BF16)
        kb.dma("qg", w1g, w1g.h[:, :].rearrange("p (j n) -> p j n", j=DCH),
               self.inp["w_1g"],
               self.inp["w_1g"].h.ap()[lay, :, :].rearrange("(j p) n -> p j n", p=128))
        w2p = kb.sb("b3w2p", [128, FCH * D], BF16)
        kb.dma("qg", w2p, w2p.h[:, :].rearrange("p (j n) -> p j n", j=FCH),
               self.inp["w_2f"],
               self.inp["w_2f"].h.ap()[lay, :, :].rearrange("(j p) n -> p j n", p=128))
        b1a = kb.sb("b3b1a", [128, FCH], F32)
        kb.dma("qy", b1a, b1a.h[:, :], self.inp["b_1a_fm"],
               self.inp["b_1a_fm"].h.ap()[lay, :, :])
        b1g = kb.sb("b3b1g", [128, FCH], F32)
        kb.dma("qy", b1g, b1g.h[:, :], self.inp["b_1g_fm"],
               self.inp["b_1g_fm"].h.ap()[lay, :, :])
        b2f = kb.sb("b3b2f", [128, DCH], F32)
        kb.dma("qy", b2f, b2f.h[:, :], self.inp["b_2f"],
               self.inp["b_2f"].h.ap()[lay, :, :])
        lgb = kb.sb("b3lgb", [128, 2 * DCH], F32)
        kb.dma("qy", lgb, lgb.h[:, :], self.inp["w_lngb_f"],
               self.inp["w_lngb_f"].h.ap()[lay, :, :])

        mz = kb.sb("b3mz", [128, DCH * 129], BF16)
        zsq = kb.sb("b3zsq", [128, 128], F32)
        rowb = kb.sb("b3row", [128, 2 * 128], F32)
        rows = kb.sb("b3rows", [1, 4 * 128], F32)
        hnf = kb.sb("b3hnf", [128, DCH * 512], BF16)
        uu = kb.sb("b3uu", [128, FCH * 512], BF16)
        gg = kb.sb("b3gg", [128, 512], BF16)
        zo = kb.sb("b3zo", [128, DCH * 128], BF16)

        for grp in range(nmlp // 4):
            for sub in range(4):
                i = grp * 4 + sub
                for j in range(DCH):
                    kb.dma("qg", mz, mz.h[:, j * 129 + 1: j * 129 + 129],
                           za_d, za_d.h.ap()[i + 1, j, :, :])
                    kb.dma("qg", mz, mz.h[:, j * 129: j * 129 + 1],
                           za_d, za_d.h.ap()[i, j, :, 127:128],
                           allow_slow_non_contiguous=True)
                psr = self.psB[0]
                for j in range(DCH):
                    off = 0 if j >= DCH // 2 else 1
                    kb.op("t", (lambda j=j, off=off: lambda e: e.matmul(
                        psr.h[0:1, 0:128], self.onescol.h[:, 0:1],
                        mz.h[:, j * 129 + off: j * 129 + off + 128],
                        start=(j == 0), stop=(j == DCH - 1)))(),
                        reads=(self.onescol, mz), writes=(psr,))
                for j in range(DCH):
                    off = 0 if j >= DCH // 2 else 1
                    kb.op("s", (lambda j=j, off=off: lambda e: e.activation(
                        zsq.h[:, :], mz.h[:, j * 129 + off: j * 129 + off + 128],
                        AF.Square))(), reads=(mz,), writes=(zsq,))
                    kb.op("t", (lambda j=j: lambda e: e.matmul(
                        psr.h[0:1, 128:256], self.onescolf.h[:, 0:1], zsq.h[:, :],
                        start=(j == 0), stop=(j == DCH - 1)))(),
                        reads=(self.onescolf, zsq), writes=(psr,))
                kb.op("v", lambda e: e.tensor_scalar(
                    rows.h[0:1, 0:128], psr.h[0:1, 0:128], 1.0 / D, None, ALU.mult),
                    reads=(psr,), writes=(rows,))
                kb.op("v", lambda e: e.tensor_tensor(
                    rows.h[0:1, 128:256], rows.h[0:1, 0:128], rows.h[0:1, 0:128],
                    ALU.mult), reads=(rows,), writes=(rows,))
                kb.op("v", lambda e: e.tensor_scalar(
                    rows.h[0:1, 256:384], psr.h[0:1, 128:256], 1.0 / D, None,
                    ALU.mult), reads=(psr,), writes=(rows,))
                kb.op("v", lambda e: e.tensor_tensor(
                    rows.h[0:1, 256:384], rows.h[0:1, 256:384], rows.h[0:1, 128:256],
                    ALU.subtract), reads=(rows,), writes=(rows,))
                kb.op("s", lambda e: e.activation(
                    rows.h[0:1, 256:384], rows.h[0:1, 256:384], AF.Sqrt,
                    bias=self.epst.h[0:1, :]),
                    reads=(rows, self.epst), writes=(rows,))
                kb.op("v", lambda e: e.reciprocal(
                    rows.h[0:1, 384:512], rows.h[0:1, 256:384]),
                    reads=(rows,), writes=(rows,))
                psmu = self.psB[1]
                kb.op("v", lambda e: e.tensor_copy(rows.h[0:1, 128:256],
                                                   rows.h[0:1, 0:128]),
                      reads=(rows,), writes=(rows,))
                kb.op("t", lambda e: e.matmul(
                    psmu.h[:, 0:128], self.ones_r.h[0:1, 0:128],
                    rows.h[0:1, 128:256], start=True, stop=False),
                    reads=(self.ones_r, rows), writes=(psmu,)) if False else None
                kb.op("t", lambda e: e.matmul(
                    psmu.h[:, 0:128], self.ones_rf.h[0:1, :],
                    rows.h[0:1, 0:128], start=True, stop=True),
                    reads=(self.ones_rf, rows), writes=(psmu,))
                kb.op("t", lambda e: e.matmul(
                    psmu.h[:, 128:256], self.ones_rf.h[0:1, :],
                    rows.h[0:1, 384:512], start=True, stop=True),
                    reads=(self.ones_rf, rows), writes=(psmu,))
                kb.op("v", lambda e: e.tensor_copy(rowb.h[:, :],
                                                   psmu.h[:, 0:256]),
                      reads=(psmu,), writes=(rowb,))
                for j in range(DCH):
                    off = 0 if j >= DCH // 2 else 1
                    kb.op("v", (lambda j=j, off=off, sub=sub: lambda e: e.tensor_sub(
                        gg.h[:, 0:128], mz.h[:, j * 129 + off: j * 129 + off + 128],
                        rowb.h[:, 0:128]))(), reads=(mz, rowb), writes=(gg,))
                    kb.op("v", (lambda j=j, sub=sub: lambda e: e.tensor_mul(
                        gg.h[:, 128:256], gg.h[:, 0:128], rowb.h[:, 128:256]))(),
                        reads=(gg, rowb), writes=(gg,))
                    kb.op("v", (lambda j=j, sub=sub: lambda e: e.tensor_scalar(
                        hnf.h[:, j * 512 + sub * 128: j * 512 + sub * 128 + 128],
                        gg.h[:, 128:256], lgb.h[:, j:j + 1],
                        lgb.h[:, DCH + j:DCH + j + 1], ALU.mult, ALU.add))(),
                        reads=(gg, lgb), writes=(hnf,))
            for fc in range(FCH):
                psa = self.psA[0]
                psg = self.psA[1]
                for j in range(DCH):
                    kb.op("t", (lambda fc=fc, j=j, psa=psa: lambda e: e.matmul(
                        psa.h[:, :], w1a.h[:, j * HID + fc * 128: j * HID + fc * 128 + 128],
                        hnf.h[:, j * 512:(j + 1) * 512],
                        start=(j == 0), stop=(j == DCH - 1)))(),
                        reads=(w1a, hnf), writes=(psa,))
                for j in range(DCH):
                    kb.op("t", (lambda fc=fc, j=j, psg=psg: lambda e: e.matmul(
                        psg.h[:, :], w1g.h[:, j * HID + fc * 128: j * HID + fc * 128 + 128],
                        hnf.h[:, j * 512:(j + 1) * 512],
                        start=(j == 0), stop=(j == DCH - 1)))(),
                        reads=(w1g, hnf), writes=(psg,))
                kb.op("s", (lambda fc=fc, psg=psg: lambda e: e.activation(
                    gg.h[:, :], psg.h[:, :], AF.Gelu, bias=b1g.h[:, fc:fc + 1]))(),
                    reads=(psg, b1g), writes=(gg,))
                kb.op("v", (lambda fc=fc, psa=psa: lambda e: e.scalar_tensor_tensor(
                    uu.h[:, fc * 512:(fc + 1) * 512], psa.h[:, :],
                    b1a.h[:, fc:fc + 1], gg.h[:, :], ALU.add, ALU.mult))(),
                    reads=(psa, b1a, gg), writes=(uu,))
            for sub in range(4):
                i = grp * 4 + sub
                if i < out_skip:
                    continue
                for j in range(DCH):
                    kb.dma("qg", mz, mz.h[:, j * 129 + 1: j * 129 + 129],
                           za_d, za_d.h.ap()[i + 1, j, :, :])
                for dc in range(DCH):
                    ps = self.psA[2 + dc % 2]
                    for fc in range(FCH):
                        kb.op("t", (lambda dc=dc, fc=fc, sub=sub, ps=ps: lambda e: e.matmul(
                            ps.h[:, 0:128],
                            w2p.h[:, fc * D + dc * 128: fc * D + dc * 128 + 128],
                            uu.h[:, fc * 512 + sub * 128: fc * 512 + sub * 128 + 128],
                            start=(fc == 0), stop=(fc == FCH - 1)))(),
                            reads=(w2p, uu), writes=(ps,))
                    kb.op("v", (lambda dc=dc, ps=ps: lambda e: e.tensor_scalar(
                        zo.h[:, dc * 128:(dc + 1) * 128], ps.h[:, 0:128],
                        b2f.h[:, dc:dc + 1], None, ALU.add))(),
                        reads=(ps, b2f), writes=(zo,))
                    kb.op("v", (lambda dc=dc: lambda e: e.tensor_add(
                        zo.h[:, dc * 128:(dc + 1) * 128],
                        zo.h[:, dc * 128:(dc + 1) * 128],
                        mz.h[:, dc * 129 + 1: dc * 129 + 129]))(),
                        reads=(zo, mz), writes=(zo,))
                kb.dma("qg", out_d,
                       out_d.h.ap()[i - out_skip, :, :, :].rearrange("j p c2 -> p j c2"),
                       zo, zo.h[:, :].rearrange("p (j c2) -> p j c2", j=DCH))
        kb.barrier()
        kb.free(w1a, w1g, w2p, b1a, b1g, b2f, lgb, mz, zsq, rowb, rows, hnf,
                uu, gg, zo)

    def emit_b4(self):
        kb = self.kb
        z2f = kb.sb("b4z", [128, DCH * 128], BF16)
        z2t = kb.sb("b4zt", [128, D], BF16)
        xt = kb.sb("b4x", [128, D], F32)
        st_ = kb.sb("b4s", [128, D], BF16)
        sg = kb.sb("b4sg", [128, D], F32)
        for tc in range(self.nch):
            i = tc % 2
            for j in range(DCH):
                kb.dma("qg", z2f, z2f.h[:, j * 128:(j + 1) * 128],
                       self.d_z2f, self.d_z2f.h.ap()[tc, j, :, :])
            pst = self.psT[i]
            for j in range(DCH):
                kb.op("t", (lambda j=j, pst=pst: lambda e: e.transpose(
                    pst.h[:, j * 128:(j + 1) * 128], z2f.h[:, j * 128:(j + 1) * 128],
                    self.id128.h[:, :]))(),
                    reads=(z2f, self.id128), writes=(pst,))
            for j in range(DCH):
                kb.op("v", (lambda j=j, pst=pst: lambda e: e.tensor_copy(
                    z2t.h[:, j * 128:(j + 1) * 128], pst.h[:, j * 128:(j + 1) * 128]))(),
                    reads=(pst,), writes=(z2t,))
            kb.op("s", lambda e: e.activation(z2t.h[:, :], z2t.h[:, :], AF.Silu),
                  reads=(z2t,), writes=(z2t,), raw_self=True)
            kb.dma("qy", xt, xt.h[:, :], self.inp["yq"],
                   self.inp["yq"].h.ap()[tc * 128:(tc + 1) * 128, :])
            kb.dma("qy", st_, st_.h[:, :], self.d_s1,
                   self.d_s1.h.ap()[tc * 128:(tc + 1) * 128, :])
            kb.op("v", lambda e: e.tensor_mul(sg.h[:, :], st_.h[:, :], z2t.h[:, :]),
                  reads=(st_, z2t), writes=(sg,))
            kb.op("v", lambda e: e.tensor_add(sg.h[:, :], sg.h[:, :], xt.h[:, :]),
                  reads=(sg, xt), writes=(sg,))
            kb.dma("qy", self.out_y, self.out_y.h.ap()[tc * 128:(tc + 1) * 128, :],
                   sg, sg.h[:, :])


def build_l1(nown=2048, npfx=2048, phases=("b1", "b2a", "b2b"), dbg=()):
    nc = bass.Bass("TRN2", target_bir_lowering=False)
    kb = KB(nc)
    l1 = L1(kb, nown, npfx)
    for name in dbg:
        b = getattr(l1, name)
        ob = kb.dram("dbg_" + name, list(b.h.shape), b.h.dtype, kind="ExternalOutput")
        setattr(l1, name + "_dbg", ob)
    l1.emit_init()
    if "b1" in phases:
        l1.emit_b1()
    if "b2a" in phases:
        l1.emit_b2a()
    if "b2b" in phases:
        l1.emit_b2b()
    if "b3" in phases:
        l1.emit_b3()
    if "b4" in phases:
        l1.emit_b4()
    for name in dbg:
        b = getattr(l1, name)
        ob = getattr(l1, name + "_dbg")
        kb.dma("qg", ob, ob.h.ap()[...], b, b.h.ap()[...])
    kb.finalize()
    return nc, l1


def _rotary_tables(pos):
    DIM_HEAD, WINDOW, n = 64, 512, 4096
    half = np.arange(0, DIM_HEAD, 2, dtype=np.float32)
    inv_freq = 1.0 / 10000.0 ** (half / DIM_HEAD)
    t = pos.astype(np.float32)
    fr = t[:, None] * inv_freq[None, :]
    freqs = np.concatenate([fr, fr], -1)
    base = (half + 0.4 * DIM_HEAD) / (1.4 * DIM_HEAD)
    power = (t - n // 2) / float(WINDOW // 2)
    scale = base[None, :] ** power[:, None]
    scale = np.concatenate([scale, scale], -1)
    cos, sin = np.cos(freqs), np.sin(freqs)
    sign = np.where(np.arange(64) < 32, -1.0, 1.0).astype(np.float32)
    return (np.float32(cos * scale), np.float32(sin * scale * sign),
            np.float32(cos / scale), np.float32(sin / scale * sign))


def _loc_masks():
    mid = np.zeros((8, 128, 512), np.float32)
    first = np.zeros((8, 128, 512), np.float32)
    for kc in range(8):
        jj = np.arange(128)[:, None] + (kc - 4) * 128
        i = np.arange(512)[None, :]
        ok = (jj <= i) & (i <= jj + 512)
        mid[kc] = np.where(ok, 0.0, -1e30)
        first[kc] = np.where(ok & (kc >= 4), 0.0, -1e30)
    return mid, first


def l1_core_inputs(I, xnew, b, h, nown=2048, npfx=2048):
    own0 = h * 2048
    yq = np.zeros((nown + 128, D), np.float32)
    yq[:nown] = xnew[b, own0:own0 + nown]
    if h == 0:
        yq[nown:nown + 2] = xnew[b, 2048:2050]
    if h == 0 or npfx == 0:
        yp = np.zeros((max(npfx, 128), D), np.float32)
    else:
        yp = np.ascontiguousarray(xnew[b, 0:npfx])
    fm16 = lambda v: np.ascontiguousarray(np.asarray(v, np.float32).reshape(16, 128).T)
    cw = np.asarray(I["L1_attn_conv_w"], np.float32)
    wconv = np.zeros((128, 4 * 16), np.float32)
    for k in range(4):
        wconv[:, k * 16:(k + 1) * 16] = cw[:, 0, k].reshape(16, 128).T
    mc = np.where(np.arange(128)[:, None] <= np.arange(128)[None, :], 0.0, -1e30).astype(np.float32)
    pos = np.arange(own0 - 1536, own0 + 2048)
    posc = np.clip(pos, 0, None)
    cq, sq, ck, sk = [np.ascontiguousarray(np.tile(t, (1, 8))) for t in _rotary_tables(posc)]
    mmid, mfirst = _loc_masks()
    ml0 = np.stack([mfirst if (own0 - 1024 + 512 * w) == 0 else mmid for w in range(6)])
    ml1 = np.stack([mfirst if (own0 - 512 + 512 * w) == 0 else mmid for w in range(5)])
    wqkv = np.asarray(I["L1_ffn_wqkv"], np.float32)
    w1f = np.asarray(I["L1_ffn_w1"], np.float32)
    w2f = np.asarray(I["L1_ffn_w2"], np.float32)
    w1a = np.zeros((2, D, HID), np.float32); w1a[:, :, :2730] = w1f[:, :, :2730]
    w1g = np.zeros((2, D, HID), np.float32); w1g[:, :, :2730] = w1f[:, :, 2730:]
    b1f = np.asarray(I["L1_ffn_b1"], np.float32)
    b1a = np.zeros((2, HID), np.float32); b1a[:, :2730] = b1f[:, :2730]
    b1g = np.zeros((2, HID), np.float32); b1g[:, :2730] = b1f[:, 2730:]
    w2p = np.zeros((2, HID, D), np.float32); w2p[:, :2730] = w2f
    fmF = lambda v: np.ascontiguousarray(np.asarray(v, np.float32).reshape(FCH, 128).T)
    d = {
        "yq": yq, "yp": yp,
        "c_id128": np.eye(128, dtype=np.float32),
        "hflag": np.full((128, 1), float(h), np.float32),
        "rflag": np.full((128, 1), 1.0 - float(h), np.float32),
        "w_in": np.asarray(I["L1_attn_in_w"], np.float32).T,
        "b_in_x": np.asarray(I["L1_attn_in_b"], np.float32)[None, :DI],
        "b_in_r_fm": fm16(np.asarray(I["L1_attn_in_b"])[DI:]),
        "w_conv_fm": wconv,
        "b_conv_fm": fm16(I["L1_attn_conv_b"]),
        "w_dt": np.asarray(I["L1_attn_dt_w"], np.float32).T,
        "b_dt": np.asarray(I["L1_attn_dt_b"], np.float32)[:, None],
        "w_xg": np.ascontiguousarray(np.asarray(I["L1_attn_xproj_w"], np.float32)[20].reshape(16, 128).T),
        "b_xg": np.asarray(I["L1_attn_xproj_b"], np.float32)[20:21][:, None],
        "v_A": np.asarray(I["L1_attn_A"], np.float32)[:, None],
        "w_Bm_fm": np.asarray(I["L1_attn_Bm"], np.float32).reshape(16, 128, 16).transpose(1, 0, 2).reshape(128, 256),
        "w_Cm_fm": np.asarray(I["L1_attn_Cm"], np.float32).reshape(16, 128, 16).transpose(1, 0, 2).reshape(128, 256),
        "w_BmT": np.asarray(I["L1_attn_Bm"], np.float32).T,
        "w_CmT": np.asarray(I["L1_attn_Cm"], np.float32).T,
        "w_out": np.asarray(I["L1_attn_out_w"], np.float32).T,
        "b_out": np.asarray(I["L1_attn_out_b"], np.float32)[None, :],
        "m_caus": mc,
        "w_qkv_f": wqkv,
        "w_wo_f": np.asarray(I["L1_ffn_wo"], np.float32),
        "w_lngb_f": np.stack([np.concatenate([_fmvec(g), _fmvec(bb)], 1)
                              for g, bb in zip(np.asarray(I["L1_ffn_ln_g"]),
                                               np.asarray(I["L1_ffn_ln_b"]))]),
        "w_1a": w1a, "w_1g": w1g,
        "b_1a_fm": np.stack([fmF(v) for v in b1a]),
        "b_1g_fm": np.stack([fmF(v) for v in b1g]),
        "w_2f": w2p,
        "b_2f": np.stack([_fmvec(v) for v in np.asarray(I["L1_ffn_b2"], np.float32)]),
        "qs_row": np.asarray(I["L1_ffn_q_scale"], np.float32),
        "ks_row": np.asarray(I["L1_ffn_k_scale"], np.float32),
        "t_cosq": cq, "t_sinq": sq, "t_cosk": ck, "t_sink": sk,
        "m_loc0": ml0, "m_loc1": ml1,
    }
    return {k: np.ascontiguousarray(v) for k, v in d.items()}


_CACHE = {}


def kernel(**inputs):
    if "l0" not in _CACHE:
        _CACHE["l0"] = build_l0(phases=("a1", "a2", "a3", "a4", "a5", "a6", "a7"))
    if "l1" not in _CACHE:
        _CACHE["l1"] = build_l1(phases=("b1", "b2a", "b2b", "b3", "b4"))
    nc0, _ = _CACHE["l0"]
    nc1, _ = _CACHE["l1"]
    cores = list(range(8))
    maps_a = [l0_core_inputs(inputs, b, h) for b in range(4) for h in range(2)]
    res_a = run_bass_kernel_spmd(nc0, maps_a, core_ids=cores)
    xnew = np.empty((4, 4096, D), np.float32)
    for ci in range(8):
        b, h = ci // 2, ci % 2
        xnew[b, h * 2048:(h + 1) * 2048] = res_a.results[ci]["xnew"]
    maps_b = [l1_core_inputs(inputs, xnew, b, h) for b in range(4) for h in range(2)]
    res_b = run_bass_kernel_spmd(nc1, maps_b, core_ids=cores)
    out = np.empty((4, 4096, D), np.float32)
    for ci in range(8):
        b, h = ci // 2, ci % 2
        out[b, h * 2048:(h + 1) * 2048] = res_b.results[ci]["yout"]
    return out
